# version 24
# baseline (speedup 1.0000x reference)
import math
import contextlib
import numpy as np
import ml_dtypes
import concourse.bass as bass
import concourse.mybir as mybir
from concourse.alu_op_type import AluOpType
from concourse.aot_env import allow_ici_all_to_all
from concourse.bass_utils import run_bass_kernel_spmd

F32 = mybir.dt.float32
BF16 = mybir.dt.bfloat16
U8 = mybir.dt.uint8
AF = mybir.ActivationFunctionType
AX = mybir.AxisListType
OP = AluOpType

D = 4096
S = 8192
HD = 128
NCORES = 8
NCHUNK = 16
TPC = 1024
SCALE = HD ** -0.5
EPS = 1e-5
NEG = -1.0e9


def chunk_rank(i):
    return min(i, NCHUNK - 1 - i)


def chunk_half(i):
    return 0 if i < NCHUNK // 2 else 1


class ESem:
    def __init__(self, h):
        self.h = h
        self.n = 0

    def inc(self):
        self.n += 1
        return self.n


class DSem:
    def __init__(self, h):
        self.h = h
        self.n = 0

    def inc(self):
        self.n += 16
        return self.n


def build_program(debug=False):
    nc = bass.Bass("TRN2", target_bir_lowering=False, debug=False)

    x_my = nc.dram_tensor("x_my", [TPC, D], F32, kind="ExternalInput").ap()
    wq_my = nc.dram_tensor("wq_my", [D, 512], F32, kind="ExternalInput").ap()
    wk_my = nc.dram_tensor("wk_my", [D, HD], F32, kind="ExternalInput").ap()
    wv_my = nc.dram_tensor("wv_my", [D, HD], F32, kind="ExternalInput").ap()
    wo = nc.dram_tensor("wo", [D, D], F32, kind="ExternalInput").ap()
    wbase = nc.dram_tensor("wbase", [D, D], F32, kind="ExternalInput").ap()
    bbase = nc.dram_tensor("bbase", [1, D], F32, kind="ExternalInput").ap()
    enc01 = nc.dram_tensor("enc01", [40, D], F32, kind="ExternalInput").ap()
    coefs = nc.dram_tensor("coefs", [128, 3840], F32, kind="ExternalInput").ap()
    masks = nc.dram_tensor("masks", [128, 3840], U8, kind="ExternalInput").ap()
    ft = nc.dram_tensor("ft", [D, D], BF16, kind="ExternalInput").ap()
    ident = nc.dram_tensor("ident", [128, 128], BF16, kind="ExternalInput").ap()
    maskc = nc.dram_tensor("maskc", [128, 2048], BF16, kind="ExternalInput").ap()
    out_my = nc.dram_tensor("out_my", [TPC, D], F32, kind="ExternalOutput").ap()

    xt_c = [nc.dram_tensor(f"xt_c{h}", [D, 512], BF16) for h in range(2)]
    xg = [nc.dram_tensor(f"xg{h}", [NCORES * D, 512], BF16,
                         addr_space="Shared") for h in range(2)]
    dk = {"kind": "ExternalOutput"} if debug else {}
    qt_d = nc.dram_tensor("qt_d", [512, S], BF16, **dk)
    oa_in = [nc.dram_tensor(f"oa_in{h}", [D, 512], BF16) for h in range(2)]
    oa_out = [nc.dram_tensor(f"oa_out{h}", [D, 512], BF16) for h in range(2)]
    h1pre_d = nc.dram_tensor("h1pre_d", [TPC, D], F32, **dk)
    h1_d = nc.dram_tensor("h1_d", [TPC, D], F32, **dk)
    h2_d = nc.dram_tensor("h2_d", [TPC, D], F32, **dk)
    if debug:
        dbg_xg = nc.dram_tensor("dbg_xg", [NCORES * D, TPC], BF16, kind="ExternalOutput")
        dbg_oin = nc.dram_tensor("dbg_oin", [D, TPC], BF16, kind="ExternalOutput")
        dbg_oout = nc.dram_tensor("dbg_oout", [D, TPC], BF16, kind="ExternalOutput")
        dbg_kv = nc.dram_tensor("dbg_kv", [128, 16384], BF16, kind="ExternalOutput")
        dbg_wq = nc.dram_tensor("dbg_wq", [128, 24576], BF16, kind="ExternalOutput")
        dbg_aux = nc.dram_tensor("dbg_aux", [128, 4096], F32, kind="ExternalOutput")
        dbg_xgs = nc.dram_tensor("dbg_xgs", [128, 16384], BF16, kind="ExternalOutput")
        dbg_g01 = nc.dram_tensor("dbg_g01", [40, 4096], BF16, kind="ExternalOutput")
        dbg_enc = nc.dram_tensor("dbg_enc", [128, 2560], BF16, kind="ExternalOutput")
        dbg_w01 = nc.dram_tensor("dbg_w01", [128, 1280], BF16, kind="ExternalOutput")

    EM = {"t": [], "v": [], "s": [], "g": [], "y": []}
    sv_e01_final = [0]

    def emit(e, fn):
        EM[e].append(fn)

    def op(e, name, *args, waits=(), inc=None, **kw):
        if inc is not None:
            inc.inc()
        wl = tuple((sm.h, (v if callable(v) else int(v))) for (sm, v) in waits)

        def fn(eng, name=name, args=args, kw=kw, wl=wl, inc=inc):
            for h, v in wl:
                eng.wait_ge(h, int(v() if callable(v) else v))
            ins = getattr(eng, name)(*args, **kw)
            if inc is not None:
                ins.then_inc(inc.h, 16 if isinstance(inc, DSem) else 1)
        EM[e].append(fn)
        return inc.n if inc is not None else None

    def dma(e, out, in_, *, waits=(), inc=None, **kw):
        return op(e, "dma_start", waits=waits, inc=inc, out=out, in_=in_, **kw)

    stack = contextlib.ExitStack()

    def sbuf(name, shape, dt):
        return stack.enter_context(nc.sbuf_tensor(name, shape, dt)).ap()

    def psum(name, shape, dt):
        return stack.enter_context(nc.psum_tensor(name, shape, dt)).ap()

    def sem(name, kind):
        return kind(stack.enter_context(nc.semaphore(name)))

    with stack:
        A = sbuf("arena_a", [128, 65536], U8)
        B = sbuf("arena_b", [128, 65536], U8)
        C = sbuf("arena_c", [128, 32768], U8)
        Dx = sbuf("arena_d", [128, 14336], U8)
        Abf = A.bitcast(BF16)
        Af = A.bitcast(F32)
        Bbf = B.bitcast(BF16)
        Bf = B.bitcast(F32)
        Bu8 = B
        Cbf = C.bitcast(BF16)
        Cf = C.bitcast(F32)
        Dbf = Dx.bitcast(BF16)
        Df = Dx.bitcast(F32)

        I_bf = sbuf("I_bf", [128, 128], BF16)
        maskc_sb = sbuf("maskc_sb", [128, 2048], BF16)
        ones_cb = sbuf("ones_cb", [128, 1], BF16)
        ones_cf = sbuf("ones_cf", [128, 1], F32)
        ones_rb = sbuf("ones_rb", [1, 128], BF16)
        ones_rf = sbuf("ones_rf", [1, 128], F32)
        b_bf = sbuf("b_bf", [1, D], BF16)
        encT_sb = sbuf("encT_sb", [128, 1280], BF16)
        e01T_sb = sbuf("e01T_sb", [40, TPC], BF16)
        w01s_sb = sbuf("w01s_sb", [128, 1280], BF16)
        g01_sb = sbuf("g01_sb", [40, D], BF16)
        stats1 = sbuf("stats1", [128, 64], F32)
        stats2 = sbuf("stats2", [128, 64], F32)
        lns = sbuf("lns", [128, 96], F32)
        sc1 = sbuf("sc1", [1, 16], F32)
        qn_t = sbuf("qn_t", [1, 64], F32)
        kn_t = sbuf("kn_t", [1, 16], F32)
        kn_t2 = sbuf("kn_t2", [1, 16], F32)
        mg2_t = sbuf("mg2_t", [1, 64], F32)
        sqrt_t = sbuf("sqrt_t", [1, 64], F32)
        negmg_t = sbuf("negmg_t", [1, 64], F32)
        recip_sb = sbuf("recip_sb", [1, 512], F32)
        negmg_sb = sbuf("negmg_sb", [128, 64], F32)

        pq = [psum(f"pq{i}", [128, 1024], F32) for i in range(4)]
        pb = [pq[i // 2][:, (i % 2) * 512:((i % 2) + 1) * 512] for i in range(8)]

        st = sem("st", ESem)
        sv = sem("sv", ESem)
        sc = sem("sc", ESem)
        gx = [sem(f"gx{i}", DSem) for i in range(2)]
        gmisc = sem("gmisc", DSem)
        gft = [sem(f"gft{i}", DSem) for i in range(3)]
        gxg = [sem(f"gxg{i}", DSem) for i in range(2)]
        gxgy = [sem(f"gxgy{i}", DSem) for i in range(2)]
        gwo = [sem(f"gwo{i}", DSem) for i in range(2)]
        gwb = [sem(f"gwb{i}", DSem) for i in range(2)]
        gw0 = sem("gw0", DSem)
        yct = sem("yct", DSem)
        yqtw = sem("yqtw", DSem)
        yqt = [sem(f"yqt{i}", DSem) for i in range(2)]
        yot = [sem(f"yot{i}", DSem) for i in range(2)]
        yotl = sem("yotl", DSem)
        yx = [sem(f"yx{i}", DSem) for i in range(2)]
        yh1prew = sem("yh1prew", DSem)
        yh1pr = [sem(f"yh1pr{i}", DSem) for i in range(2)]
        yh1w = sem("yh1w", DSem)
        yh1s = [sem(f"yh1s{i}", DSem) for i in range(2)]
        yh2w = sem("yh2w", DSem)
        yh2r = [sem(f"yh2r{i}", DSem) for i in range(4)]
        yout = sem("yout", DSem)
        cc1 = sem("cc1", ESem)
        cc2 = [sem(f"cc2_{i}", ESem) for i in range(2)]

        A3 = Abf.rearrange("p (dq t) -> p dq t", dq=32)
        B3 = Bbf.rearrange("p (dq t) -> p dq t", dq=32)

        dma("g", I_bf[:, :], ident, inc=gmisc)
        dma("g", maskc_sb[:, :], maskc, inc=gmisc)
        dma("g", b_bf[:, :], bbase, inc=gmisc)
        dma("g", Cbf[0:40, 8192:12288], enc01, inc=gmisc)
        dma("g", Bf[:, 0:3840], coefs, inc=gmisc)
        dma("g", Bu8[:, 15360:19200], masks, inc=gmisc)
        gmisc_all = gmisc.n

        op("v", "memset", ones_cb[:, :], 1.0, inc=sv)
        op("v", "memset", ones_cf[:, :], 1.0, inc=sv)
        op("v", "memset", ones_rb[:, :], 1.0, inc=sv)
        op("v", "memset", ones_rf[:, :], 1.0, inc=sv)
        memset_done = sv.n

        FTS = (0, 4096, 12288)
        st_g01 = {}

        def emit_ft_load(dq):
            sl = FTS[dq % 3]
            w = []
            if dq >= 3:
                w.append((st, lambda d=dq: st_g01[d - 3]))
            dma("y", Cbf[:, sl:sl + 4096],
                ft[dq * 128:(dq + 1) * 128, :], waits=w, inc=gft[dq % 3])

        st_xgrp = {}
        sv_xevac = {}
        for tt in range(8):
            sl = tt % 2
            w = []
            if tt < 2:
                w.append((gmisc, gmisc_all))
            if tt >= 2:
                w.append((st, st_xgrp[(tt - 2, 7)]))
            dma("g", Bbf[:, 20480 + sl * 4096:20480 + (sl + 1) * 4096],
                x_my[tt * 128:(tt + 1) * 128, :], waits=w, inc=gx[sl])
            for ggi in range(8):
                bank = ggi % 2
                w = [(gx[sl], gx[sl].n)]
                gi = tt * 8 + ggi
                if gi >= 2:
                    pr = gi - 2
                    w.append((sv, sv_xevac[(pr // 8, pr % 8)]))
                if gi < 2:
                    w.append((gmisc, gmisc_all))
                for k in range(4):
                    dq = ggi * 4 + k
                    op("t", "matmul", pb[bank][:, k * 128:(k + 1) * 128],
                       Bbf[:, 20480 + sl * 4096 + dq * 128:
                           20480 + sl * 4096 + (dq + 1) * 128],
                       I_bf[:, :], start=True, stop=True,
                       waits=w if k == 0 else (), inc=st if k == 3 else None)
                st_xgrp[(tt, ggi)] = st.n
                dstap = A3[:, ggi * 4:(ggi + 1) * 4, tt * 128:(tt + 1) * 128]
                op("v", "tensor_copy", dstap, pb[bank][:, :],
                   waits=[(st, st_xgrp[(tt, ggi)])], inc=sv)
                sv_xevac[(tt, ggi)] = sv.n
        sv_xT_all = sv.n

        yct_h = {}
        A3c = Abf.rearrange("p (dq t) -> p dq t", dq=32)
        for h in range(2):
            for dqg in range(4):
                w = [(sv, sv_xevac[(4 * h + 3, 2 * dqg + 1)])]
                dma("y", xt_c[h].ap()[dqg * 1024:(dqg + 1) * 1024, :]
                    .rearrange("(dq p) t -> p dq t", p=128),
                    A3c[:, dqg * 8:(dqg + 1) * 8, h * 512:(h + 1) * 512],
                    waits=w, inc=yct)
            yct_h[h] = yct.n

        def agfn(eng):
            for h in range(2):
                eng.wait_ge(yct.h, yct_h[h])
                eng.collective_compute(
                    "AllGather", OP.bypass,
                    replica_groups=[list(range(NCORES))],
                    ins=[xt_c[h].ap().opt()], outs=[xg[h].ap().opt()],
                ).then_inc(cc1.h, 1)
        emit("g", agfn)
        cc1.n = 2

        for dq in range(32):
            emit_ft_load(dq)

        mf = Bf[:, 4800:8640]
        accw = Bf[:, 8640:9920]
        for k in range(96):
            msk = Bu8[:, 15360 + k * 40:15360 + (k + 1) * 40]
            op("v", "tensor_copy", mf[:, k * 40:(k + 1) * 40], msk,
               waits=[(gmisc, gmisc_all)] if k == 0 else ())
        for k in range(96):
            cof = Bf[:, k * 40:(k + 1) * 40]
            op("v", "tensor_tensor", mf[:, k * 40:(k + 1) * 40], cof,
               mf[:, k * 40:(k + 1) * 40], OP.mult)
        for dq in range(32):
            op("v", "tensor_tensor", accw[:, dq * 40:(dq + 1) * 40],
               mf[:, dq * 40:(dq + 1) * 40],
               mf[:, (32 + dq) * 40:(33 + dq) * 40], OP.add)
        for dq in range(32):
            op("v", "tensor_tensor", accw[:, dq * 40:(dq + 1) * 40],
               accw[:, dq * 40:(dq + 1) * 40],
               mf[:, (64 + dq) * 40:(65 + dq) * 40], OP.add)
        for dq in range(32):
            op("v", "tensor_copy", w01s_sb[:, dq * 40:(dq + 1) * 40],
               accw[:, dq * 40:(dq + 1) * 40],
               inc=sv if dq == 31 else None)
        sv_w01 = sv.n

        cnt = {0: 0, 1: 0, 2: 0}
        ft_target = {}
        for dq in range(32):
            cnt[dq % 3] += 16
            ft_target[dq] = cnt[dq % 3]
        for dq in range(32):
            sl = FTS[dq % 3]
            for oc in range(8):
                w = [(gft[dq % 3], ft_target[dq])]
                if dq == 0:
                    w.append((sv, sv_w01))
                op("t", "matmul", pb[oc][0:40, :],
                   w01s_sb[:, dq * 40:(dq + 1) * 40],
                   Cbf[:, sl + oc * 512:sl + (oc + 1) * 512],
                   start=(dq == 0), stop=(dq == 31),
                   waits=w if oc == 0 else (), inc=st if oc == 7 else None)
            st_g01[dq] = st.n
        for oc in range(8):
            op("v", "tensor_scalar", g01_sb[0:40, oc * 512:(oc + 1) * 512],
               pb[oc][0:40, :], 0.5, None, OP.mult,
               waits=[(st, st_g01[31])] if oc == 0 else (), inc=sv)
        sv_g01 = sv.n

        st_encT = {}
        for dq in range(32):
            bank = dq % 2
            w = [(sv, sv_g01)] if dq < 2 else [(sv, sv.n)]
            op("t", "matmul", pb[bank][:, 0:40],
               Cbf[0:40, 8192 + dq * 128:8192 + (dq + 1) * 128], I_bf[0:40, 0:40],
               start=True, stop=True, waits=w, inc=st)
            st_encT[dq] = st.n
            op("v", "tensor_copy", encT_sb[:, dq * 40:(dq + 1) * 40],
               pb[bank][:, 0:40], waits=[(st, st_encT[dq])], inc=sv)
        sv_encT = sv.n

        def Kt(j):
            return Cbf[:, j * 128:(j + 1) * 128]

        def Vt_tile(j):
            return Cbf[:, 8192 + j * 128:8192 + (j + 1) * 128]

        wk_off = 16384
        wv_off = 16384 + 4096
        dma("g", Abf.rearrange("p (dq q) -> p dq q", dq=64)[:, 0:32, 0:512],
            wq_my.rearrange("(dq p) q -> p dq q", p=128),
            waits=[(yct, yct_h[1])], inc=gw0)
        dma("g", Abf[:, wk_off:wk_off + 4096].rearrange("p (dq q) -> p dq q", dq=32),
            wk_my.rearrange("(dq p) q -> p dq q", p=128), inc=gw0)
        dma("g", Abf[:, wv_off:wv_off + 4096].rearrange("p (dq q) -> p dq q", dq=32),
            wv_my.rearrange("(dq p) q -> p dq q", p=128), inc=gw0)
        gw_all = gw0.n
        if debug:
            dma("y", dbg_wq.ap(), Abf[:, 0:24576],
                waits=[(gw0, gw_all)], inc=yout)

        st_idx_cons = {}
        sv_p0c_evac = {}
        sc_qstage = [0, 0]
        pend_qn = []
        st_qn_ms = {}
        sv_qn_ms = {}

        def flush_qn():
            qg, sq_ms, cell = pend_qn.pop(0)
            w = [(sc, sq_ms)]
            if st_qn_ms:
                w.append((sv, sv_qn_ms[max(sv_qn_ms)]))
            op("t", "matmul", pb[7][0:1, :], ones_cf[:, :],
               Df[:, 1536 + (qg % 2) * 512:2048 + (qg % 2) * 512],
               start=True, stop=True, waits=w, inc=st)
            st_qn_ms[qg] = st.n
            op("v", "reduce_max", qn_t[0:1, cell:cell + 1], pb[7][0:1, :],
               AX.X, waits=[(st, st.n)], inc=sv)
            sv_qn_ms[qg] = sv.n
        st_vtr = {}
        sv_vt = {}
        idx_order = [i for i in range(16) if i % 2 == 0] + \
                    [i for i in range(16) if i % 2 == 1]
        for pos, idx in enumerate(idx_order):
            blk, tc = idx // 2, idx % 2
            gch = blk if tc == 0 else (NCHUNK - 1 - blk)
            sl = pos % 2
            w = [(cc1, 1 + tc)]
            if pos == 0:
                w.append((sv, sv_w01))
                w.append((st, st_xgrp[(7, 7)]))
            if pos >= 2:
                w.append((st, st_idx_cons[pos - 2]))
            for sub in range(4):
                rows = xg[tc].ap()[blk * D + sub * 1024:
                                   blk * D + (sub + 1) * 1024, :]
                dma("g" if sub < 2 else "y",
                    Bbf[:, sl * 16384 + sub * 4096: sl * 16384 + (sub + 1) * 4096]
                    .rearrange("p (dq t) -> p dq t", dq=8),
                    rows.rearrange("(dq p) t -> p dq t", p=128),
                    waits=w if sub in (0, 2) else (),
                    inc=gxg[sl] if sub < 2 else gxgy[sl])

            def xgt(dq, sl=sl):
                return Bbf[:, sl * 16384 + dq * 512: sl * 16384 + (dq + 1) * 512]

            wfirst = [(gxg[sl], gxg[sl].n), (gxgy[sl], gxgy[sl].n), (gw0, gw_all)]
            if pos >= 1:
                wfirst.append((sv, sv_p0c_evac[pos - 1]))
            else:
                wfirst.append((sv, sv_encT))
            stops = {}
            for grp in range(6):
                for dq in range(32):
                    if grp < 4:
                        lhs = Abf[:, dq * 512 + grp * 128: dq * 512 + (grp + 1) * 128]
                        bank = grp
                    elif grp == 4:
                        lhs = Abf[:, wk_off + dq * 128: wk_off + (dq + 1) * 128]
                        bank = 4
                    else:
                        lhs = Abf[:, wv_off + dq * 128: wv_off + (dq + 1) * 128]
                        bank = 5
                    op("t", "matmul", pb[bank][:, :], lhs, xgt(dq),
                       start=(dq == 0), stop=(dq == 31),
                       waits=wfirst if (grp == 0 and dq == 0) else (),
                       inc=st if dq == 31 else None)
                stops[grp] = st.n
            st_idx_cons[pos] = st.n
            for qh in range(4):
                qsl = qh % 2
                w = [(st, stops[qh])]
                if pos > 0 or qh >= 2:
                    w.append((yqtw, yqtw.n))
                    w.append((sc, sc_qstage[qsl]))
                op("v", "tensor_copy", Dbf[:, qsl * 512:(qsl + 1) * 512],
                   pb[qh][:, :], waits=w, inc=sv)
                sv_qev = sv.n
                dma("y", qt_d.ap()[qh * 128:(qh + 1) * 128, gch * 512:(gch + 1) * 512],
                    Dbf[:, qsl * 512:(qsl + 1) * 512],
                    waits=[(sv, sv_qev)], inc=yqtw)
                qg = pos * 4 + qh
                qslot = qg % 2
                w = [(sv, sv_qev)]
                if qg >= 2:
                    w.append((st, lambda g=qg: st_qn_ms[g - 2]))
                op("s", "activation", Df[:, 1536 + qslot * 512:2048 + qslot * 512],
                   Dbf[:, qsl * 512:(qsl + 1) * 512], AF.Square, waits=w, inc=sc)
                sc_qstage[qsl] = sc.n
                pend_qn.append((qg, sc.n, qh * 16 + gch))
                while pend_qn and pend_qn[0][0] <= qg - 2:
                    flush_qn()
            op("v", "tensor_copy", Cbf[:, gch * 512:(gch + 1) * 512],
               pb[4][:, :], waits=[(st, stops[4])], inc=sv)
            vsl = pos % 2
            w = [(st, stops[5])]
            if pos >= 2:
                w.append((st, st_vtr[pos - 2]))
            op("v", "tensor_copy", Dbf[:, 1024 + vsl * 512:1024 + (vsl + 1) * 512],
               pb[5][:, :], waits=w, inc=sv)
            sv_vt[pos] = sv.n
            w = [(sv, sv_vt[pos])]
            if pos >= 1:
                w.append((sv, sv_p0c_evac[pos - 1]))
            for k in range(4):
                op("t", "matmul", pb[6][:, k * 128:(k + 1) * 128],
                   Dbf[:, 1024 + vsl * 512 + k * 128:1024 + vsl * 512 + (k + 1) * 128],
                   I_bf[:, :], start=True, stop=True,
                   waits=w if k == 0 else (), inc=st if k == 3 else None)
            st_vtr[pos] = st.n
            op("v", "tensor_copy", Cbf[:, 8192 + gch * 512:8192 + (gch + 1) * 512],
               pb[6][:, :], waits=[(st, st_vtr[pos])], inc=sv)
            sv_p0c_evac[pos] = sv.n
        sv_kv_all = sv.n

        while pend_qn:
            flush_qn()
        st_kn = {}
        for slc in range(16):
            w = [(sv, sv_kv_all)] if slc == 0 else [(st, st_kn[slc - 1])]
            op("s", "activation", Df[:, 1024:1536],
               Cbf[:, slc * 512:(slc + 1) * 512], AF.Square, waits=w, inc=sc)
            w = [(sc, sc.n)]
            if slc == 0:
                w.append((sv, memset_done))
                w.append((sv, sv_qn_ms[max(sv_qn_ms)]))
            else:
                w.append((sv, sv.n))
            op("t", "matmul", pb[7][0:1, :], ones_cf[:, :], Df[:, 1024:1536],
               start=True, stop=True, waits=w, inc=st)
            st_kn[slc] = st.n
            op("v", "reduce_max", kn_t[0:1, slc:slc + 1], pb[7][0:1, :], AX.X,
               waits=[(st, st_kn[slc])], inc=sv)
        st_kn15 = st_kn[15]
        op("s", "activation", kn_t2[0:1, :], kn_t[0:1, :], AF.Identity,
           waits=[(sv, sv.n)], inc=sc)
        op("v", "reduce_max", sc1[0:1, 8:9], kn_t2[0:1, :], AX.X,
           waits=[(sc, sc.n)], inc=sv)
        op("s", "activation", sc1[0:1, 10:11], sc1[0:1, 8:9], AF.Identity,
           waits=[(sv, sv.n)], inc=sc)
        op("v", "tensor_scalar", mg2_t[0:1, :], qn_t[0:1, :], sc1[0:1, 10:11],
           None, OP.mult, waits=[(sc, sc.n)], inc=sv)
        op("s", "activation", sqrt_t[0:1, :], mg2_t[0:1, :], AF.Sqrt,
           waits=[(sv, sv.n)], inc=sc)
        op("v", "tensor_scalar", negmg_t[0:1, :], sqrt_t[0:1, :], -SCALE,
           None, OP.mult, waits=[(sc, sc.n)], inc=sv)
        op("t", "matmul", pb[7][:, 0:64], ones_rf[:, :], negmg_t[0:1, :],
           start=True, stop=True, waits=[(sv, sv.n)], inc=st)
        op("v", "tensor_copy", negmg_sb[:, :], pb[7][:, 0:64],
           waits=[(st, st.n)], inc=sv)
        sv_kmax = sv.n
        if debug:
            dma("y", dbg_kv.ap(), Cbf[:, :],
                waits=[(sv, sv_kv_all)], inc=yout)
            dma("y", dbg_xgs.ap(), Bbf[:, 0:16384],
                waits=[(gxg[0], 64)], inc=yout)
            dma("y", dbg_g01.ap(), g01_sb[0:40, :], waits=[(sv, sv_g01)], inc=yout)
            dma("y", dbg_enc.ap()[:, 0:1280], encT_sb[:, :], waits=[(sv, sv_encT)], inc=yout)
            dma("y", dbg_w01.ap(), w01s_sb[:, :], waits=[(sv, sv_w01)], inc=yout)

        def p2_sb(jj):
            return Dbf[:, (jj % 2) * 1024:((jj % 2) + 1) * 1024]

        def qt_sl(p):
            return Dbf[:, 2048 + (p % 2) * 512:2048 + ((p % 2) + 1) * 512]

        def ot_sl(p):
            return Dbf[:, 3072 + (p % 2) * 512:3072 + ((p % 2) + 1) * 512]

        bc_sb = Df[:, 3072:3584]

        gwo_ms = {}
        for ocg in range(2):
            wsl = ocg % 2
            w = [(st, st_idx_cons[15])] if ocg == 0 else []
            for sub in range(4):
                rows = wo[sub * 1024:(sub + 1) * 1024, ocg * 512:(ocg + 1) * 512]
                dma("g", Abf[:, wsl * 16384 + sub * 4096: wsl * 16384 + (sub + 1) * 4096]
                    .rearrange("p (dq t) -> p dq t", dq=8),
                    rows.rearrange("(dq p) t -> p dq t", p=128),
                    waits=w if sub == 0 else (), inc=gwo[wsl])
                w = []
            gwo_ms[ocg] = gwo[wsl].n

        def a2a_h(h, waitval):
            def fn(eng, h=h, waitval=waitval):
                eng.wait_ge(yot[h].h, waitval)
                with allow_ici_all_to_all():
                    eng.collective_compute(
                        "AllToAll", OP.bypass,
                        replica_groups=[list(range(NCORES))],
                        ins=[oa_in[h].ap().opt()], outs=[oa_out[h].ap().opt()],
                    ).then_inc(cc2[h].h, 1)
            emit("g", fn)
            cc2[h].n = 1

        st_qk = {}
        st_sums = {}
        sc_exp2 = {}
        sv_omult = {}
        st_lastqk = {}
        sc_bc = {}
        sv_negmg = {}
        ot_dma = {}
        sv_recip = {}
        st_bc = {}
        gk = 0
        order = [(i, qh) for i in range(NCHUNK) for qh in range(4)]
        epilog_done = set()

        def flush_epilog(s2, force=False):
            tgt = s2 if force else s2 - 1
            if tgt < 0 or tgt in epilog_done:
                return
            epilog_done.add(tgt)
            i2, qh2 = order[tgt]
            ob2 = 4 + tgt % 2
            sb2 = 6 + tgt % 2
            w = [(sv, sv_recip[tgt])]
            op("t", "matmul", pb[sb2][:, :], ones_rf[:, :], recip_sb[0:1, :],
               start=True, stop=True, waits=w, inc=st)
            st_bc[tgt] = st.n
            op("s", "activation", bc_sb, pb[sb2][:, :], AF.Identity,
               waits=[(st, st_bc[tgt])], inc=sc)
            sc_bc[tgt] = sc.n
            w = [(sc, sc_bc[tgt])]
            if tgt >= 2:
                w.append(ot_dma[tgt - 2])
            op("v", "tensor_tensor", ot_sl(tgt), pb[ob2][:, :], bc_sb,
               OP.mult, waits=w, inc=sv)
            sv_omult[tgt] = sv.n
            d_i, h_i = chunk_rank(i2), chunk_half(i2)
            dma("y", oa_in[h_i].ap()[d_i * 512 + qh2 * 128:
                                     d_i * 512 + (qh2 + 1) * 128, :],
                ot_sl(tgt), waits=[(sv, sv_omult[tgt])], inc=yot[h_i])
            ot_dma[tgt] = (yot[h_i], yot[h_i].n)
            if tgt == 31:
                a2a_h(0, yot[0].n)
        for seq, (i, qh) in enumerate(order):
            pairid = qh * 16 + i
            psl = seq % 2
            ob = 4 + seq % 2
            sb_ = 6 + seq % 2
            nk = 4 * i + 4
            w = []
            if seq < 2:
                w.append((yqtw, yqtw.n))
                w.append((st, st_kn15))
            if seq >= 2:
                w.append((st, st_lastqk[seq - 2]))
            dma("y", qt_sl(seq),
                qt_d.ap()[qh * 128:(qh + 1) * 128, i * 512:(i + 1) * 512],
                waits=w, inc=yqt[psl])

            g0 = gk
            nd = nk // 2

            def emit_qk2(jj, g0=g0, seq=seq, psl=psl, i=i):
                for hh in (0, 1):
                    j = 2 * jj + hh
                    g = g0 + j
                    dst = pq[jj % 2][:, hh * 512:(hh + 1) * 512]
                    w = []
                    if jj <= 1 and hh == 0:
                        if jj * 2 + g0 >= 4:
                            w.append((sc, sc_exp2[(g0 + 2 * jj) // 2 - 2]))
                        if seq == 0 and jj == 1:
                            w.append((sv, sv_kmax))
                        if j == 0:
                            w.append((yqt[psl], yqt[psl].n))
                            if seq >= 2:
                                w.append((sv, sv_omult[seq - 2]))
                            if seq == 0:
                                w.append((sv, sv_kv_all))
                    diag = (j >= 4 * i)
                    op("t", "matmul", dst, Kt(j), qt_sl(seq),
                       start=True, stop=(not diag), waits=w,
                       inc=None if diag else st)
                    if diag:
                        v_ = j - 4 * i
                        op("t", "matmul", dst, I_bf[:, :],
                           maskc_sb[:, v_ * 512:(v_ + 1) * 512],
                           start=False, stop=True, inc=st)
                    st_qk[g] = st.n

            emit_qk2(0)
            if nd > 1:
                emit_qk2(1)
            for jj in range(nd):
                g2 = g0 // 2 + jj
                gA = g0 + 2 * jj
                gB = gA + 1
                w = [(st, st_qk[gB])]
                if seq == 0 and jj == 0:
                    w.append((sv, sv_kmax))
                if g2 >= 2:
                    w.append((st, st_sums.get(g0 + 2 * (jj - 2) + 1, 0)))
                op("s", "activation", p2_sb(jj), pq[jj % 2][:, :], AF.Exp,
                   bias=negmg_sb[:, pairid:pairid + 1], scale=SCALE,
                   waits=w, inc=sc)
                sc_exp2[g2] = sc.n
                w = [(sc, sc_exp2[g2])]
                op("t", "matmul", pb[ob][:, :], Vt_tile(2 * jj),
                   p2_sb(jj)[:, 0:512], start=(jj == 0), stop=False,
                   skip_group_check=True, waits=w)
                op("t", "matmul", pb[ob][:, :], Vt_tile(2 * jj + 1),
                   p2_sb(jj)[:, 512:1024], start=False, stop=(jj == nd - 1),
                   skip_group_check=True)
                op("t", "matmul", pb[sb_][0:1, :], ones_cb[:, :],
                   p2_sb(jj)[:, 0:512], start=(jj == 0), stop=False,
                   skip_group_check=True)
                op("t", "matmul", pb[sb_][0:1, :], ones_cb[:, :],
                   p2_sb(jj)[:, 512:1024], start=False, stop=(jj == nd - 1),
                   skip_group_check=True, inc=st)
                st_sums[gB] = st.n
                if jj + 2 < nd:
                    emit_qk2(jj + 2)
                if jj == 0:
                    flush_epilog(seq)
            gk = g0 + nk
            st_lastqk[seq] = st_qk[gk - 1]

            w = [(st, st_sums[gk - 1])]
            if seq >= 1:
                w.append((st, st_bc[seq - 1]))
            op("v", "reciprocal", recip_sb[0:1, :], pb[sb_][0:1, :],
               waits=w, inc=sv)
            sv_recip[seq] = sv.n
        flush_epilog(len(order) - 1, force=True)
        a2a_h(1, yot[1].n)
        st_lastP1 = st.n
        sc_lastP1 = sc.n
        sv_lastP1 = sv.n

        x_f = lambda p: Df[:, (p % 2) * 512:((p % 2) + 1) * 512]
        h1p_t = lambda p: Df[:, 1024 + (p % 2) * 512:1024 + ((p % 2) + 1) * 512]
        sqs = Df[:, 2048:2560]

        st_ymm = {}
        sv_yevac = {}
        sc_ysq = {}
        for half in range(2):
            OTb = (Bbf[:, 16384:32768] if half == 0 else Bbf[:, 0:16384])
            w = [(cc2[half], 1)]
            for sub in range(4):
                rows = oa_out[half].ap()[sub * 1024:(sub + 1) * 1024, :]
                dma("y", OTb[:, sub * 4096:(sub + 1) * 4096]
                    .rearrange("p (fq t) -> p fq t", fq=8),
                    rows.rearrange("(fq p) t -> p fq t", p=128),
                    waits=w if sub == 0 else (), inc=yotl)
                w = []
            ot_l = yotl.n
            for oc in range(8):
                ocg = half * 8 + oc
                wsl = ocg % 2
                if ocg >= 2:
                    w = [(st, st_ymm[(half, oc - 2, 3)] if oc >= 2
                          else st_ymm[(half - 1, oc + 6, 3)])]
                    for sub in range(4):
                        rows = wo[sub * 1024:(sub + 1) * 1024,
                                  oc * 512:(oc + 1) * 512]
                        dma("g", Abf[:, wsl * 16384 + sub * 4096:
                                     wsl * 16384 + (sub + 1) * 4096]
                            .rearrange("p (dq t) -> p dq t", dq=8),
                            rows.rearrange("(dq p) t -> p dq t", p=128),
                            waits=w if sub == 0 else (), inc=gwo[wsl])
                        w = []
                    gwo_ms[ocg] = gwo[wsl].n
                for tt in range(4):
                    tg = half * 4 + tt
                    it = ocg * 4 + tt
                    par = it % 2
                    w = []
                    if it < 2:
                        w.append((st, st_lastP1))
                    if it >= 2:
                        w.append((sv, sv_yevac[it - 2]))
                    dma("y", x_f(par),
                        x_my[tg * 128:(tg + 1) * 128, oc * 512:(oc + 1) * 512],
                        waits=w, inc=yx[par])
                    bank = tt % 2
                    w = [(gwo[wsl], gwo_ms[ocg]), (yotl, ot_l)]
                    if it >= 2:
                        w.append((sv, sv_yevac[it - 2]))
                    if ocg == 0 and tt < 2:
                        w.append((sc, sc_lastP1))
                    for fq in range(32):
                        op("t", "matmul", pb[bank][:, :],
                           OTb[:, fq * 512 + tt * 128: fq * 512 + (tt + 1) * 128],
                           Abf[:, wsl * 16384 + fq * 512:
                               wsl * 16384 + (fq + 1) * 512],
                           start=(fq == 0), stop=(fq == 31),
                           waits=w if fq == 0 else (), inc=st if fq == 31 else None)
                    st_ymm[(half, oc, tt)] = st.n
                    w = [(st, st_ymm[(half, oc, tt)]), (yx[par], yx[par].n)]
                    if it >= 2:
                        w.append((yh1prew, yh1prew.n))
                        w.append((sc, sc_ysq.get(it - 2, 0)))
                    op("v", "tensor_tensor", h1p_t(par), pb[bank][:, :],
                       x_f(par), OP.add, waits=w, inc=sv)
                    sv_yevac[it] = sv.n
                    col = tg * 8 + oc
                    w = [(sv, sv_yevac[it]), (sc, sc_ysq.get(it - 1, 0))]
                    op("s", "activation", sqs, h1p_t(par), AF.Identity,
                       accum_out=stats1[:, col:col + 1], waits=w, inc=sc)
                    op("s", "activation", sqs, h1p_t(par), AF.Square,
                       accum_out=stats2[:, col:col + 1], inc=sc)
                    sc_ysq[it] = sc.n
                    dma("y", h1pre_d.ap()[tg * 128:(tg + 1) * 128,
                                          oc * 512:(oc + 1) * 512],
                        h1p_t(par), waits=[(sv, sv_yevac[it])], inc=yh1prew)
        yh1prew_all = yh1prew.n


        def ln_batch_finalize(first_waits):
            w = list(first_waits)
            for tg in range(8):
                op("v", "tensor_reduce", lns[:, tg:tg + 1],
                   stats1[:, tg * 8:(tg + 1) * 8], AX.X, OP.add,
                   waits=w if tg == 0 else ())
                w = []
                op("v", "tensor_reduce", lns[:, 8 + tg:9 + tg],
                   stats2[:, tg * 8:(tg + 1) * 8], AX.X, OP.add,
                   inc=sv if tg == 7 else None)
            op("s", "mul", lns[:, 16:24], lns[:, 0:8], 1.0 / D,
               waits=[(sv, sv.n)], inc=sc)
            op("s", "mul", lns[:, 24:32], lns[:, 8:16], 1.0 / D, inc=sc)
            op("s", "activation", lns[:, 32:40], lns[:, 16:24], AF.Square,
               inc=sc)
            op("v", "scalar_tensor_tensor", lns[:, 40:48], lns[:, 24:32], EPS,
               lns[:, 32:40], OP.add, OP.subtract, waits=[(sc, sc.n)], inc=sv)
            op("s", "activation", lns[:, 48:56], lns[:, 40:48], AF.Sqrt,
               waits=[(sv, sv.n)], inc=sc)
            op("v", "reciprocal", lns[:, 56:64], lns[:, 48:56],
               waits=[(sc, sc.n)], inc=sv)
            op("s", "activation", lns[:, 64:72], lns[:, 56:64], AF.Identity,
               waits=[(sv, sv.n)], inc=sc)
            op("s", "mul", lns[:, 72:80], lns[:, 16:24], -1.0, inc=sc)
            op("v", "tensor_tensor", lns[:, 80:88], lns[:, 64:72],
               lns[:, 72:80], OP.mult, waits=[(sc, sc.n)], inc=sv)
            op("s", "activation", lns[:, 88:89], lns[:, 80:81], AF.Identity,
               waits=[(sv, sv.n)], inc=sc)
            return sc.n, sv.n
        h1in = lambda p: Af[:, (p % 2) * 4096:((p % 2) + 1) * 4096]
        h1o = lambda p: Af[:, 8192 + (p % 2) * 4096:8192 + ((p % 2) + 1) * 4096]
        h1bf = lambda p: (Dbf[:, 0:4096] if p % 2 == 0 else Cbf[:, 0:4096])

        sc_fin1, sv_fin1 = ln_batch_finalize(
            [(sc, sc_ysq[63]), (sv, sv_lastP1)])
        sc_apply = {}
        st_h1t = {}
        sv_h1grp = {}
        for tg in range(8):
            par = tg % 2
            w = [(yh1prew, yh1prew_all)]
            if tg == 0:
                w.append((st, st_ymm[(1, 7, 3)]))
            if tg >= 2:
                w.append((sc, sc_apply[tg - 2]))
            dma("y", h1in(par), h1pre_d.ap()[tg * 128:(tg + 1) * 128, :],
                waits=w, inc=yh1pr[par])
            w = [(sv, sv_fin1), (sc, sc_fin1), (yh1pr[par], yh1pr[par].n)]
            if tg >= 2:
                w.append((sv, sv_h1grp.get((tg - 2, 7), 0)))
                w.append((st, st_h1t[tg - 2]))
            op("s", "activation", h1bf(tg), h1in(par), AF.Identity,
               bias=lns[:, 80 + tg:81 + tg], scale=lns[:, 64 + tg:65 + tg],
               waits=w, inc=sc)
            sc_apply[tg] = sc.n
            sv_h1bf = sc_apply[tg]
            for gg in range(8):
                bank = 2 + gg % 2
                tgg = tg * 8 + gg
                w = [(sc, sc_apply[tg])]
                if tgg >= 2:
                    w.append((sv, sv_h1grp[(tgg - 2) // 8, (tgg - 2) % 8]))
                if tgg < 2:
                    w.append((sv, sv_omult[63]))
                for k in range(4):
                    dq = gg * 4 + k
                    op("t", "matmul", pb[bank][:, k * 128:(k + 1) * 128],
                       h1bf(tg)[:, dq * 128:(dq + 1) * 128], I_bf[:, :],
                       start=True, stop=True, waits=w if k == 0 else (),
                       inc=st if k == 3 else None)
                stg = st.n
                dstap = B3[:, gg * 4:(gg + 1) * 4, tg * 128:(tg + 1) * 128]
                op("v", "tensor_copy", dstap, pb[bank][:, :],
                   waits=[(st, stg)], inc=sv)
                sv_h1grp[(tg, gg)] = sv.n
            st_h1t[tg] = st.n
        sv_h1T_all = sv.n

        h1s_t = lambda p: Df[:, (p % 2) * 512:((p % 2) + 1) * 512]
        h2_t = lambda p: Df[:, 1024 + (p % 2) * 512:1024 + ((p % 2) + 1) * 512]
        h1f_t = lambda p: Df[:, 2560 + (p % 2) * 512:2560 + ((p % 2) + 1) * 512]

        st_e01 = {}
        sv_e01l = 0
        for tc2 in range(2):
            for dq in range(32):
                w = []
                if dq == 0 and tc2 == 0:
                    w.append((sv, sv_h1T_all))
                if dq == 0 and tc2 == 1:
                    w.append((sv, sv_e01l))
                op("t", "matmul", pb[4][0:40, :],
                   encT_sb[:, dq * 40:(dq + 1) * 40],
                   Bbf[:, dq * 1024 + tc2 * 512: dq * 1024 + (tc2 + 1) * 512],
                   start=(dq == 0), stop=(dq == 31),
                   waits=w, inc=st if dq == 31 else None)
            st_e01[tc2] = st.n
            op("v", "tensor_copy", e01T_sb[0:40, tc2 * 512:(tc2 + 1) * 512],
               pb[4][0:40, :], waits=[(st, st_e01[tc2])], inc=sv)
            sv_e01l = sv.n
            sv_e01_final[0] = sv.n
        sv_e01 = sv.n

        st_base = {}
        sv_h2evac = {}
        sc_h2sq = {}
        st_wbt = {}
        sv_wbtevac = {}
        for oc in range(8):
            wsl = (oc + 1) % 2
            w = []
            if oc == 0:
                w.append((st, st_ymm[(1, 7, 3)]))
            if oc == 1:
                w.append((sc, sc_apply[7]))
            if oc >= 2:
                w.append((st, st_wbt[oc - 2]))
            dma("g", Abf[:, wsl * 16384:(wsl + 1) * 16384]
                .rearrange("p (ot d) -> p ot d", ot=4),
                wbase[oc * 512:(oc + 1) * 512, :]
                .rearrange("(ot p) d -> p ot d", p=128),
                waits=w, inc=gwb[wsl])
            for dq in range(32):
                bank = 2 + dq % 2
                w = [(gwb[wsl], gwb[wsl].n)]
                if oc == 0 and dq == 0:
                    w.append((sv, sv_e01))
                if oc >= 1 and dq == 0:
                    w.append((st, st_base[(oc - 1, 7)]))
                if dq >= 2:
                    w.append((sv, sv_wbtevac[(oc, dq - 2)]))
                for k in range(4):
                    op("t", "matmul", pb[bank][:, k * 128:(k + 1) * 128],
                       Abf[:, wsl * 16384 + k * 4096 + dq * 128:
                           wsl * 16384 + k * 4096 + (dq + 1) * 128],
                       I_bf[:, :], start=True, stop=True,
                       waits=w if k == 0 else (), inc=st if k == 3 else None)
                stg = st.n
                op("v", "tensor_copy", Cbf[:, dq * 512:(dq + 1) * 512],
                   pb[bank][:, :], waits=[(st, stg)], inc=sv)
                sv_wbtevac[(oc, dq)] = sv.n
            st_wbt[oc] = st.n
            sv_wbt_all = sv.n
            for tt in range(8):
                it = oc * 8 + tt
                par = it % 2
                w = []
                if it == 0:
                    w.append((yh1prew, yh1prew.n))
                if it < 2:
                    w.append((st, st_h1t[7]))
                if it >= 2:
                    w.append((sv, sv_h2evac[it - 2]))
                dma("y", h1s_t(par),
                    h1pre_d.ap()[tt * 128:(tt + 1) * 128, oc * 512:(oc + 1) * 512],
                    waits=w, inc=yh1s[par])
                w = [(yh1s[par], yh1s[par].n)]
                if it >= 2:
                    w.append((sv, sv_h2evac[it - 2]))
                op("s", "activation", h1f_t(par), h1s_t(par), AF.Identity,
                   bias=lns[:, 80 + tt:81 + tt], scale=lns[:, 64 + tt:65 + tt],
                   waits=w, inc=sc)
                sc_h1f = sc.n
                bank = tt % 2
                w = [(sv, sv_wbt_all)]
                if it >= 2:
                    w.append((sv, sv_h2evac[it - 2]))
                for dq in range(32):
                    op("t", "matmul", pb[bank][:, :],
                       Bbf[:, dq * 1024 + tt * 128: dq * 1024 + (tt + 1) * 128],
                       Cbf[:, dq * 512:(dq + 1) * 512],
                       start=(dq == 0), stop=False,
                       waits=w if dq == 0 else ())
                op("t", "matmul", pb[bank][:, :], ones_rb[:, :],
                   b_bf[0:1, oc * 512:(oc + 1) * 512], start=False, stop=False)
                op("t", "matmul", pb[bank][:, :],
                   e01T_sb[0:40, tt * 128:(tt + 1) * 128],
                   g01_sb[0:40, oc * 512:(oc + 1) * 512],
                   start=False, stop=True, inc=st)
                st_base[(oc, tt)] = st.n
                w = [(st, st_base[(oc, tt)]), (sc, sc_h1f)]
                if it >= 2:
                    w.append((yh2w, yh2w.n))
                    w.append((sc, sc_h2sq.get(it - 2, 0)))
                op("v", "tensor_tensor", h2_t(par), pb[bank][:, :], h1f_t(par),
                   OP.add, waits=w, inc=sv)
                sv_h2evac[it] = sv.n
                col = tt * 8 + oc
                w = [(sv, sv_h2evac[it]), (sc, sc_h2sq.get(it - 1, 0))]
                op("s", "activation", sqs, h2_t(par), AF.Identity,
                   accum_out=stats1[:, col:col + 1], waits=w, inc=sc)
                op("s", "activation", sqs, h2_t(par), AF.Square,
                   accum_out=stats2[:, col:col + 1], inc=sc)
                sc_h2sq[it] = sc.n
                dma("y", h2_d.ap()[tt * 128:(tt + 1) * 128,
                                   oc * 512:(oc + 1) * 512],
                    h2_t(par), waits=[(sv, sv_h2evac[it])], inc=yh2w)
        yh2w_all = yh2w.n

        h2in = lambda p: Af[:, (p % 4) * 4096:((p % 4) + 1) * 4096]
        outsb = lambda p: Cf[:, (p % 2) * 4096:((p % 2) + 1) * 4096]
        sc_fin2, sv_fin2 = ln_batch_finalize(
            [(sc, sc_h2sq[63]), (sc, sc_apply[7])])
        sc_ap2 = {}
        for tg in range(8):
            par = tg % 4
            w = [(yh2w, yh2w_all)]
            if tg == 0:
                w.append((gwb[0], gwb[0].n))
                w.append((gwb[1], gwb[1].n))
            if tg >= 4:
                w.append((sc, sc_ap2[tg - 4]))
            dma("g", h2in(par), h2_d.ap()[tg * 128:(tg + 1) * 128, :],
                waits=w, inc=yh2r[par])
            w = [(sv, sv_fin2), (sc, sc_fin2), (yh2r[par], yh2r[par].n)]
            if tg >= 2:
                w.append((yout, 16 * (tg - 1)))
            op("s", "activation", outsb(tg), h2in(par), AF.Identity,
               bias=lns[:, 80 + tg:81 + tg], scale=lns[:, 64 + tg:65 + tg],
               waits=w, inc=sc)
            sc_ap2[tg] = sc.n
            dma("y", out_my[tg * 128:(tg + 1) * 128, :], outsb(tg),
                waits=[(sc, sc_ap2[tg])], inc=yout)
        if debug:
            dma("y", dbg_xg.ap()[:, 0:512], xg[0].ap(),
                waits=[(cc1, 2)], inc=yout)
            dma("y", dbg_xg.ap()[:, 512:1024], xg[1].ap(),
                waits=[(cc1, 2)], inc=yout)
            dma("y", dbg_oin.ap()[:, 0:512], oa_in[0].ap(),
                waits=[(yot[0], yot[0].n)], inc=yout)
            dma("y", dbg_oin.ap()[:, 512:1024], oa_in[1].ap(),
                waits=[(yot[1], yot[1].n)], inc=yout)
            dma("y", dbg_oout.ap()[:, 0:512], oa_out[0].ap(),
                waits=[(cc2[0], 1)], inc=yout)
            dma("y", dbg_oout.ap()[:, 512:1024], oa_out[1].ap(),
                waits=[(cc2[1], 1)], inc=yout)
            dma("y", dbg_enc.ap()[0:40, 1280:2304], e01T_sb[0:40, :],
                waits=[(sv, lambda: sv_e01_final[0])], inc=yout)
        yout_all = yout.n

        def finwait(eng):
            eng.wait_ge(yout.h, yout_all)
        emit("y", finwait)

        with nc.Block() as block:
            @block.tensor
            def _(t):
                for fn in EM["t"]:
                    fn(t)

            @block.vector
            def _(v):
                for fn in EM["v"]:
                    fn(v)

            @block.scalar
            def _(s):
                for fn in EM["s"]:
                    fn(s)

            @block.gpsimd
            def _(g):
                for fn in EM["g"]:
                    fn(g)

            @block.sync
            def _(y):
                for fn in EM["y"]:
                    fn(y)

    return nc


_CACHE = {}


def _build_ft():
    if "ft" in _CACHE:
        return _CACHE["ft"]
    n = D
    t = np.arange(n)[:, None].astype(np.float64)
    F = np.zeros((n, n), dtype=np.float64)
    F[:, 0] = 1.0 / math.sqrt(n)
    f = np.arange(1, (n - 1) // 2 + 1)[None, :].astype(np.float64)
    ang = 2.0 * np.pi * t * f / n
    F[:, 1:1 + 2 * ((n - 1) // 2):2] = math.sqrt(2.0 / n) * np.cos(ang)
    F[:, 2:2 + 2 * ((n - 1) // 2):2] = math.sqrt(2.0 / n) * np.sin(ang)
    F[:, n - 1] = ((-1.0) ** t[:, 0]) / math.sqrt(n)
    FT = np.ascontiguousarray(F.T.astype(np.float32))
    _CACHE["ft"] = FT
    return FT


def _build_maskc():
    kk = np.arange(128)[:, None]
    qq = np.arange(512)[None, :]
    out = np.zeros((128, 4 * 512), dtype=np.float32)
    for v in range(4):
        out[:, v * 512:(v + 1) * 512] = np.where(qq < kk + 128 * v, NEG, 0.0)
    return out.astype(ml_dtypes.bfloat16)


def kernel(L0_attn_x, L0_attn_wq, L0_attn_wk, L0_attn_wv, L0_attn_wo,
           L0_ffn_W_base, L0_ffn_b_base, L0_ffn_enc0, L0_ffn_enc1,
           L0_ffn_coefs0, L0_ffn_coefs1, L0_ffn_mask0, L0_ffn_mask1):
    x = np.asarray(L0_attn_x, dtype=np.float32).reshape(S, D)
    wq = np.asarray(L0_attn_wq, dtype=np.float32)
    wk = np.asarray(L0_attn_wk, dtype=np.float32)
    wv = np.asarray(L0_attn_wv, dtype=np.float32)
    wo_f = np.ascontiguousarray(np.asarray(L0_attn_wo, dtype=np.float32))
    wb_f = np.ascontiguousarray(np.asarray(L0_ffn_W_base, dtype=np.float32))
    bb = np.ascontiguousarray(np.asarray(L0_ffn_b_base, dtype=np.float32).reshape(1, D))
    enc01 = np.ascontiguousarray(
        np.concatenate([np.asarray(L0_ffn_enc0, np.float32),
                        np.asarray(L0_ffn_enc1, np.float32)], axis=0))
    c01 = np.concatenate([np.asarray(L0_ffn_coefs0, np.float32),
                          np.asarray(L0_ffn_coefs1, np.float32)], axis=2)
    m01 = np.concatenate([np.asarray(L0_ffn_mask0), np.asarray(L0_ffn_mask1)],
                         axis=2).astype(np.uint8)
    coefs_st = np.ascontiguousarray(
        c01.reshape(3, 32, 128, 40).transpose(2, 0, 1, 3).reshape(128, 3840))
    masks_st = np.ascontiguousarray(
        m01.reshape(3, 32, 128, 40).transpose(2, 0, 1, 3).reshape(128, 3840))
    ft = _build_ft().astype(ml_dtypes.bfloat16)
    ident = np.eye(128, dtype=np.float32).astype(ml_dtypes.bfloat16)
    maskc = _build_maskc()

    if "nc" not in _CACHE:
        _CACHE["nc"] = build_program()
    nc = _CACHE["nc"]

    in_maps = []
    for c in range(NCORES):
        lo, hi = c, NCHUNK - 1 - c
        x_my = np.ascontiguousarray(
            np.concatenate([x[lo * 512:(lo + 1) * 512, :],
                            x[hi * 512:(hi + 1) * 512, :]], axis=0))
        in_maps.append(dict(
            x_my=x_my,
            wq_my=np.ascontiguousarray(wq[:, c * 512:(c + 1) * 512]),
            wk_my=np.ascontiguousarray(wk[:, c * 128:(c + 1) * 128]),
            wv_my=np.ascontiguousarray(wv[:, c * 128:(c + 1) * 128]),
            wo=wo_f, wbase=wb_f, bbase=bb, enc01=enc01,
            coefs=coefs_st, masks=masks_st, ft=ft, ident=ident, maskc=maskc,
        ))
    res = run_bass_kernel_spmd(nc, in_maps, core_ids=list(range(NCORES)))
    out = np.empty((S, D), dtype=np.float32)
    for c in range(NCORES):
        o = res.results[c]["out_my"]
        lo, hi = c, NCHUNK - 1 - c
        out[lo * 512:(lo + 1) * 512, :] = o[0:512, :]
        out[hi * 512:(hi + 1) * 512, :] = o[512:1024, :]
    return out.reshape(1, S, D)


# version 25
# speedup vs baseline: 1.0401x; 1.0401x over previous
import math
import contextlib
import numpy as np
import ml_dtypes
import concourse.bass as bass
import concourse.mybir as mybir
from concourse.alu_op_type import AluOpType
from concourse.aot_env import allow_ici_all_to_all
from concourse.bass_utils import run_bass_kernel_spmd

F32 = mybir.dt.float32
BF16 = mybir.dt.bfloat16
U8 = mybir.dt.uint8
AF = mybir.ActivationFunctionType
AX = mybir.AxisListType
OP = AluOpType

D = 4096
S = 8192
HD = 128
NCORES = 8
NCHUNK = 16
TPC = 1024
SCALE = HD ** -0.5
EPS = 1e-5
NEG = -1.0e9


def chunk_rank(i):
    return min(i, NCHUNK - 1 - i)


def chunk_half(i):
    return 0 if i < NCHUNK // 2 else 1


class ESem:
    def __init__(self, h):
        self.h = h
        self.n = 0

    def inc(self):
        self.n += 1
        return self.n


class DSem:
    def __init__(self, h):
        self.h = h
        self.n = 0

    def inc(self):
        self.n += 16
        return self.n


def build_program(debug=False):
    nc = bass.Bass("TRN2", target_bir_lowering=False, debug=False)

    x_my = nc.dram_tensor("x_my", [TPC, D], F32, kind="ExternalInput").ap()
    wq_my = nc.dram_tensor("wq_my", [D, 512], F32, kind="ExternalInput").ap()
    wk_my = nc.dram_tensor("wk_my", [D, HD], F32, kind="ExternalInput").ap()
    wv_my = nc.dram_tensor("wv_my", [D, HD], F32, kind="ExternalInput").ap()
    wo = nc.dram_tensor("wo", [D, D], F32, kind="ExternalInput").ap()
    wbase = nc.dram_tensor("wbase", [D, D], F32, kind="ExternalInput").ap()
    bbase = nc.dram_tensor("bbase", [1, D], F32, kind="ExternalInput").ap()
    enc01 = nc.dram_tensor("enc01", [40, D], F32, kind="ExternalInput").ap()
    coefs = nc.dram_tensor("coefs", [128, 3840], F32, kind="ExternalInput").ap()
    masks = nc.dram_tensor("masks", [128, 3840], U8, kind="ExternalInput").ap()
    ft = nc.dram_tensor("ft", [D, D], BF16, kind="ExternalInput").ap()
    ident = nc.dram_tensor("ident", [128, 128], BF16, kind="ExternalInput").ap()
    maskc = nc.dram_tensor("maskc", [128, 2048], BF16, kind="ExternalInput").ap()
    out_my = nc.dram_tensor("out_my", [TPC, D], F32, kind="ExternalOutput").ap()

    xt_c = [nc.dram_tensor(f"xt_c{h}", [D, 512], BF16) for h in range(2)]
    xg = [nc.dram_tensor(f"xg{h}", [NCORES * D, 512], BF16,
                         addr_space="Shared") for h in range(2)]
    dk = {"kind": "ExternalOutput"} if debug else {}
    qt_d = nc.dram_tensor("qt_d", [512, S], BF16, **dk)
    oa_in = [nc.dram_tensor(f"oa_in{h}", [D, 512], BF16) for h in range(2)]
    oa_out = [nc.dram_tensor(f"oa_out{h}", [D, 512], BF16) for h in range(2)]
    h1pre_d = nc.dram_tensor("h1pre_d", [TPC, D], F32, **dk)
    h1_d = nc.dram_tensor("h1_d", [TPC, D], F32, **dk)
    h2_d = nc.dram_tensor("h2_d", [TPC, D], F32, **dk)
    if debug:
        dbg_xg = nc.dram_tensor("dbg_xg", [NCORES * D, TPC], BF16, kind="ExternalOutput")
        dbg_oin = nc.dram_tensor("dbg_oin", [D, TPC], BF16, kind="ExternalOutput")
        dbg_oout = nc.dram_tensor("dbg_oout", [D, TPC], BF16, kind="ExternalOutput")
        dbg_kv = nc.dram_tensor("dbg_kv", [128, 16384], BF16, kind="ExternalOutput")
        dbg_wq = nc.dram_tensor("dbg_wq", [128, 24576], BF16, kind="ExternalOutput")
        dbg_aux = nc.dram_tensor("dbg_aux", [128, 4096], F32, kind="ExternalOutput")
        dbg_xgs = nc.dram_tensor("dbg_xgs", [128, 16384], BF16, kind="ExternalOutput")
        dbg_g01 = nc.dram_tensor("dbg_g01", [40, 4096], BF16, kind="ExternalOutput")
        dbg_enc = nc.dram_tensor("dbg_enc", [128, 2560], BF16, kind="ExternalOutput")
        dbg_w01 = nc.dram_tensor("dbg_w01", [128, 1280], BF16, kind="ExternalOutput")

    EM = {"t": [], "v": [], "s": [], "g": [], "y": []}
    sv_e01_final = [0]

    def emit(e, fn):
        EM[e].append(fn)

    def op(e, name, *args, waits=(), inc=None, **kw):
        if inc is not None:
            inc.inc()
        wl = tuple((sm.h, (v if callable(v) else int(v))) for (sm, v) in waits)

        def fn(eng, name=name, args=args, kw=kw, wl=wl, inc=inc):
            for h, v in wl:
                eng.wait_ge(h, int(v() if callable(v) else v))
            ins = getattr(eng, name)(*args, **kw)
            if inc is not None:
                ins.then_inc(inc.h, 16 if isinstance(inc, DSem) else 1)
        EM[e].append(fn)
        return inc.n if inc is not None else None

    def dma(e, out, in_, *, waits=(), inc=None, **kw):
        return op(e, "dma_start", waits=waits, inc=inc, out=out, in_=in_, **kw)

    stack = contextlib.ExitStack()

    def sbuf(name, shape, dt):
        return stack.enter_context(nc.sbuf_tensor(name, shape, dt)).ap()

    def psum(name, shape, dt):
        return stack.enter_context(nc.psum_tensor(name, shape, dt)).ap()

    def sem(name, kind):
        return kind(stack.enter_context(nc.semaphore(name)))

    with stack:
        A = sbuf("arena_a", [128, 65536], U8)
        B = sbuf("arena_b", [128, 65536], U8)
        C = sbuf("arena_c", [128, 32768], U8)
        Dx = sbuf("arena_d", [128, 14336], U8)
        Abf = A.bitcast(BF16)
        Af = A.bitcast(F32)
        Bbf = B.bitcast(BF16)
        Bf = B.bitcast(F32)
        Bu8 = B
        Cbf = C.bitcast(BF16)
        Cf = C.bitcast(F32)
        Dbf = Dx.bitcast(BF16)
        Df = Dx.bitcast(F32)

        I_bf = sbuf("I_bf", [128, 128], BF16)
        maskc_sb = sbuf("maskc_sb", [128, 2048], BF16)
        ones_cb = sbuf("ones_cb", [128, 1], BF16)
        ones_cf = sbuf("ones_cf", [128, 1], F32)
        ones_rb = sbuf("ones_rb", [1, 128], BF16)
        ones_rf = sbuf("ones_rf", [1, 128], F32)
        b_bf = sbuf("b_bf", [1, D], BF16)
        encT_sb = sbuf("encT_sb", [128, 1280], BF16)
        e01T_sb = sbuf("e01T_sb", [40, TPC], BF16)
        w01s_sb = sbuf("w01s_sb", [128, 1280], BF16)
        g01_sb = sbuf("g01_sb", [40, D], BF16)
        stats1 = sbuf("stats1", [128, 64], F32)
        stats2 = sbuf("stats2", [128, 64], F32)
        lns = sbuf("lns", [128, 96], F32)
        sc1 = sbuf("sc1", [1, 16], F32)
        qn_t = sbuf("qn_t", [1, 64], F32)
        kn_t = sbuf("kn_t", [1, 16], F32)
        kn_t2 = sbuf("kn_t2", [1, 16], F32)
        mg2_t = sbuf("mg2_t", [1, 64], F32)
        sqrt_t = sbuf("sqrt_t", [1, 64], F32)
        negmg_t = sbuf("negmg_t", [1, 64], F32)
        recip_sb = sbuf("recip_sb", [1, 512], F32)
        negmg_sb = sbuf("negmg_sb", [128, 64], F32)

        pq = [psum(f"pq{i}", [128, 1024], F32) for i in range(4)]
        pb = [pq[i // 2][:, (i % 2) * 512:((i % 2) + 1) * 512] for i in range(8)]

        st = sem("st", ESem)
        sv = sem("sv", ESem)
        sc = sem("sc", ESem)
        gx = [sem(f"gx{i}", DSem) for i in range(2)]
        gmisc = sem("gmisc", DSem)
        gft = [sem(f"gft{i}", DSem) for i in range(3)]
        gxg = [sem(f"gxg{i}", DSem) for i in range(2)]
        gwo = [sem(f"gwo{i}", DSem) for i in range(2)]
        gwb = [sem(f"gwb{i}", DSem) for i in range(2)]
        gw0 = sem("gw0", DSem)
        yct = sem("yct", DSem)
        yqtw2 = [sem(f"yqtw{i}", DSem) for i in range(2)]
        yqt = [sem(f"yqt{i}", DSem) for i in range(2)]
        yot = [sem(f"yot{i}", DSem) for i in range(2)]
        yotl = sem("yotl", DSem)
        yx = [sem(f"yx{i}", DSem) for i in range(2)]
        yh1prew = sem("yh1prew", DSem)
        yh1pr = [sem(f"yh1pr{i}", DSem) for i in range(2)]
        yh1w = sem("yh1w", DSem)
        yh1s = [sem(f"yh1s{i}", DSem) for i in range(2)]
        yh2w = sem("yh2w", DSem)
        yh2r = [sem(f"yh2r{i}", DSem) for i in range(4)]
        yout = sem("yout", DSem)
        cc1 = sem("cc1", ESem)
        cc2 = [sem(f"cc2_{i}", ESem) for i in range(2)]

        A3 = Abf.rearrange("p (dq t) -> p dq t", dq=32)
        B3 = Bbf.rearrange("p (dq t) -> p dq t", dq=32)

        dma("g", I_bf[:, :], ident, inc=gmisc)
        dma("g", maskc_sb[:, :], maskc, inc=gmisc)
        dma("g", b_bf[:, :], bbase, inc=gmisc)
        dma("g", Cbf[0:40, 8192:12288], enc01, inc=gmisc)
        dma("g", Bf[:, 0:3840], coefs, inc=gmisc)
        dma("g", Bu8[:, 15360:19200], masks, inc=gmisc)
        gmisc_all = gmisc.n

        op("v", "memset", ones_cb[:, :], 1.0, inc=sv)
        op("v", "memset", ones_cf[:, :], 1.0, inc=sv)
        op("v", "memset", ones_rb[:, :], 1.0, inc=sv)
        op("v", "memset", ones_rf[:, :], 1.0, inc=sv)
        memset_done = sv.n

        FTS = (0, 4096, 12288)
        st_g01 = {}

        def emit_ft_load(dq):
            sl = FTS[dq % 3]
            w = []
            if dq >= 3:
                w.append((st, lambda d=dq: st_g01[d - 3]))
            dma("y", Cbf[:, sl:sl + 4096],
                ft[dq * 128:(dq + 1) * 128, :], waits=w, inc=gft[dq % 3])

        st_xgrp = {}
        sv_xevac = {}
        for tt in range(8):
            sl = tt % 2
            w = []
            if tt < 2:
                w.append((gmisc, gmisc_all))
            if tt >= 2:
                w.append((st, st_xgrp[(tt - 2, 7)]))
            dma("g", Bbf[:, 20480 + sl * 4096:20480 + (sl + 1) * 4096],
                x_my[tt * 128:(tt + 1) * 128, :], waits=w, inc=gx[sl])
            for ggi in range(8):
                bank = ggi % 2
                w = [(gx[sl], gx[sl].n)]
                gi = tt * 8 + ggi
                if gi >= 2:
                    pr = gi - 2
                    w.append((sv, sv_xevac[(pr // 8, pr % 8)]))
                if gi < 2:
                    w.append((gmisc, gmisc_all))
                for k in range(4):
                    dq = ggi * 4 + k
                    op("t", "matmul", pb[bank][:, k * 128:(k + 1) * 128],
                       Bbf[:, 20480 + sl * 4096 + dq * 128:
                           20480 + sl * 4096 + (dq + 1) * 128],
                       I_bf[:, :], start=True, stop=True,
                       waits=w if k == 0 else (), inc=st if k == 3 else None)
                st_xgrp[(tt, ggi)] = st.n
                dstap = A3[:, ggi * 4:(ggi + 1) * 4, tt * 128:(tt + 1) * 128]
                op("v", "tensor_copy", dstap, pb[bank][:, :],
                   waits=[(st, st_xgrp[(tt, ggi)])], inc=sv)
                sv_xevac[(tt, ggi)] = sv.n
        sv_xT_all = sv.n

        yct_h = {}
        A3c = Abf.rearrange("p (dq t) -> p dq t", dq=32)
        for h in range(2):
            for dqg in range(4):
                w = [(sv, sv_xevac[(4 * h + 3, 2 * dqg + 1)])]
                dma("y", xt_c[h].ap()[dqg * 1024:(dqg + 1) * 1024, :]
                    .rearrange("(dq p) t -> p dq t", p=128),
                    A3c[:, dqg * 8:(dqg + 1) * 8, h * 512:(h + 1) * 512],
                    waits=w, inc=yct)
            yct_h[h] = yct.n

        def agfn(eng):
            for h in range(2):
                eng.wait_ge(yct.h, yct_h[h])
                eng.collective_compute(
                    "AllGather", OP.bypass,
                    replica_groups=[list(range(NCORES))],
                    ins=[xt_c[h].ap().opt()], outs=[xg[h].ap().opt()],
                ).then_inc(cc1.h, 1)
        emit("g", agfn)
        cc1.n = 2

        for dq in range(32):
            emit_ft_load(dq)

        mf = Bf[:, 4800:8640]
        accw = Bf[:, 8640:9920]
        for k in range(96):
            msk = Bu8[:, 15360 + k * 40:15360 + (k + 1) * 40]
            op("v", "tensor_copy", mf[:, k * 40:(k + 1) * 40], msk,
               waits=[(gmisc, gmisc_all)] if k == 0 else ())
        for k in range(96):
            cof = Bf[:, k * 40:(k + 1) * 40]
            op("v", "tensor_tensor", mf[:, k * 40:(k + 1) * 40], cof,
               mf[:, k * 40:(k + 1) * 40], OP.mult)
        for dq in range(32):
            op("v", "tensor_tensor", accw[:, dq * 40:(dq + 1) * 40],
               mf[:, dq * 40:(dq + 1) * 40],
               mf[:, (32 + dq) * 40:(33 + dq) * 40], OP.add)
        for dq in range(32):
            op("v", "tensor_tensor", accw[:, dq * 40:(dq + 1) * 40],
               accw[:, dq * 40:(dq + 1) * 40],
               mf[:, (64 + dq) * 40:(65 + dq) * 40], OP.add)
        for dq in range(32):
            op("v", "tensor_copy", w01s_sb[:, dq * 40:(dq + 1) * 40],
               accw[:, dq * 40:(dq + 1) * 40],
               inc=sv if dq == 31 else None)
        sv_w01 = sv.n

        cnt = {0: 0, 1: 0, 2: 0}
        ft_target = {}
        for dq in range(32):
            cnt[dq % 3] += 16
            ft_target[dq] = cnt[dq % 3]
        for dq in range(32):
            sl = FTS[dq % 3]
            for oc in range(8):
                w = [(gft[dq % 3], ft_target[dq])]
                if dq == 0:
                    w.append((sv, sv_w01))
                op("t", "matmul", pb[oc][0:40, :],
                   w01s_sb[:, dq * 40:(dq + 1) * 40],
                   Cbf[:, sl + oc * 512:sl + (oc + 1) * 512],
                   start=(dq == 0), stop=(dq == 31),
                   waits=w if oc == 0 else (), inc=st if oc == 7 else None)
            st_g01[dq] = st.n
        for oc in range(8):
            op("v", "tensor_scalar", g01_sb[0:40, oc * 512:(oc + 1) * 512],
               pb[oc][0:40, :], 0.5, None, OP.mult,
               waits=[(st, st_g01[31])] if oc == 0 else (), inc=sv)
        sv_g01 = sv.n

        st_encT = {}
        for dq in range(32):
            bank = dq % 2
            w = [(sv, sv_g01)] if dq < 2 else [(sv, sv.n)]
            op("t", "matmul", pb[bank][:, 0:40],
               Cbf[0:40, 8192 + dq * 128:8192 + (dq + 1) * 128], I_bf[0:40, 0:40],
               start=True, stop=True, waits=w, inc=st)
            st_encT[dq] = st.n
            op("v", "tensor_copy", encT_sb[:, dq * 40:(dq + 1) * 40],
               pb[bank][:, 0:40], waits=[(st, st_encT[dq])], inc=sv)
        sv_encT = sv.n

        def Kt(j):
            return Cbf[:, j * 128:(j + 1) * 128]

        def Vt_tile(j):
            return Cbf[:, 8192 + j * 128:8192 + (j + 1) * 128]

        wk_off = 16384
        wv_off = 16384 + 4096
        dma("g", Abf.rearrange("p (dq q) -> p dq q", dq=64)[:, 0:32, 0:512],
            wq_my.rearrange("(dq p) q -> p dq q", p=128),
            waits=[(yct, yct_h[1])], inc=gw0)
        dma("g", Abf[:, wk_off:wk_off + 4096].rearrange("p (dq q) -> p dq q", dq=32),
            wk_my.rearrange("(dq p) q -> p dq q", p=128), inc=gw0)
        dma("g", Abf[:, wv_off:wv_off + 4096].rearrange("p (dq q) -> p dq q", dq=32),
            wv_my.rearrange("(dq p) q -> p dq q", p=128), inc=gw0)
        gw_all = gw0.n
        if debug:
            dma("y", dbg_wq.ap(), Abf[:, 0:24576],
                waits=[(gw0, gw_all)], inc=yout)

        st_idx_cons = {}
        sv_p0c_evac = {}
        sc_qstage = [0, 0]
        pend_qn = []
        st_qn_ms = {}
        sv_qn_ms = {}

        def flush_qn():
            qg, sq_ms, cell = pend_qn.pop(0)
            w = [(sc, sq_ms)]
            if st_qn_ms:
                w.append((sv, sv_qn_ms[max(sv_qn_ms)]))
            op("t", "matmul", pb[7][0:1, :], ones_cf[:, :],
               Df[:, 1536 + (qg % 2) * 512:2048 + (qg % 2) * 512],
               start=True, stop=True, waits=w, inc=st)
            st_qn_ms[qg] = st.n
            op("v", "reduce_max", qn_t[0:1, cell:cell + 1], pb[7][0:1, :],
               AX.X, waits=[(st, st.n)], inc=sv)
            sv_qn_ms[qg] = sv.n
        st_vtr = {}
        sv_vt = {}
        idx_order = [i for i in range(16) if i % 2 == 0] + \
                    [i for i in range(16) if i % 2 == 1]
        for pos, idx in enumerate(idx_order):
            blk, tc = idx // 2, idx % 2
            gch = blk if tc == 0 else (NCHUNK - 1 - blk)
            sl = pos % 2
            w = [(cc1, 1 + tc)]
            if pos == 0:
                w.append((sv, sv_w01))
                w.append((st, st_xgrp[(7, 7)]))
            if pos >= 2:
                w.append((st, st_idx_cons[pos - 2]))
            for sub in range(4):
                rows = xg[tc].ap()[blk * D + sub * 1024:
                                   blk * D + (sub + 1) * 1024, :]
                dma("g", Bbf[:, sl * 16384 + sub * 4096: sl * 16384 + (sub + 1) * 4096]
                    .rearrange("p (dq t) -> p dq t", dq=8),
                    rows.rearrange("(dq p) t -> p dq t", p=128),
                    waits=w if sub == 0 else (), inc=gxg[sl])
                w = []

            def xgt(dq, sl=sl):
                return Bbf[:, sl * 16384 + dq * 512: sl * 16384 + (dq + 1) * 512]

            wfirst = [(gxg[sl], gxg[sl].n), (gw0, gw_all)]
            if pos >= 1:
                wfirst.append((sv, sv_p0c_evac[pos - 1]))
            else:
                wfirst.append((sv, sv_encT))
            stops = {}
            for grp in range(6):
                for dq in range(32):
                    if grp < 4:
                        lhs = Abf[:, dq * 512 + grp * 128: dq * 512 + (grp + 1) * 128]
                        bank = grp
                    elif grp == 4:
                        lhs = Abf[:, wk_off + dq * 128: wk_off + (dq + 1) * 128]
                        bank = 4
                    else:
                        lhs = Abf[:, wv_off + dq * 128: wv_off + (dq + 1) * 128]
                        bank = 5
                    op("t", "matmul", pb[bank][:, :], lhs, xgt(dq),
                       start=(dq == 0), stop=(dq == 31),
                       waits=wfirst if (grp == 0 and dq == 0) else (),
                       inc=st if dq == 31 else None)
                stops[grp] = st.n
            st_idx_cons[pos] = st.n
            for qh in range(4):
                qsl = qh % 2
                w = [(st, stops[qh])]
                if pos > 0 or qh >= 2:
                    w.append((yqtw2[qsl], yqtw2[qsl].n))
                    w.append((sc, sc_qstage[qsl]))
                op("v", "tensor_copy", Dbf[:, qsl * 512:(qsl + 1) * 512],
                   pb[qh][:, :], waits=w, inc=sv)
                sv_qev = sv.n
                dma("y", qt_d.ap()[qh * 128:(qh + 1) * 128, gch * 512:(gch + 1) * 512],
                    Dbf[:, qsl * 512:(qsl + 1) * 512],
                    waits=[(sv, sv_qev)], inc=yqtw2[qsl])
                qg = pos * 4 + qh
                qslot = qg % 2
                w = [(sv, sv_qev)]
                if qg >= 2:
                    w.append((st, lambda g=qg: st_qn_ms[g - 2]))
                op("s", "activation", Df[:, 1536 + qslot * 512:2048 + qslot * 512],
                   Dbf[:, qsl * 512:(qsl + 1) * 512], AF.Square, waits=w, inc=sc)
                sc_qstage[qsl] = sc.n
                pend_qn.append((qg, sc.n, qh * 16 + gch))
                while pend_qn and pend_qn[0][0] <= qg - 2:
                    flush_qn()
            op("v", "tensor_copy", Cbf[:, gch * 512:(gch + 1) * 512],
               pb[4][:, :], waits=[(st, stops[4])], inc=sv)
            vsl = pos % 2
            w = [(st, stops[5])]
            if pos >= 2:
                w.append((st, st_vtr[pos - 2]))
            op("v", "tensor_copy", Dbf[:, 1024 + vsl * 512:1024 + (vsl + 1) * 512],
               pb[5][:, :], waits=w, inc=sv)
            sv_vt[pos] = sv.n
            w = [(sv, sv_vt[pos])]
            if pos >= 1:
                w.append((sv, sv_p0c_evac[pos - 1]))
            for k in range(4):
                op("t", "matmul", pb[6][:, k * 128:(k + 1) * 128],
                   Dbf[:, 1024 + vsl * 512 + k * 128:1024 + vsl * 512 + (k + 1) * 128],
                   I_bf[:, :], start=True, stop=True,
                   waits=w if k == 0 else (), inc=st if k == 3 else None)
            st_vtr[pos] = st.n
            op("v", "tensor_copy", Cbf[:, 8192 + gch * 512:8192 + (gch + 1) * 512],
               pb[6][:, :], waits=[(st, st_vtr[pos])], inc=sv)
            sv_p0c_evac[pos] = sv.n
        sv_kv_all = sv.n

        while pend_qn:
            flush_qn()
        st_kn = {}
        for slc in range(16):
            w = [(sv, sv_kv_all)] if slc == 0 else [(st, st_kn[slc - 1])]
            op("s", "activation", Df[:, 1024:1536],
               Cbf[:, slc * 512:(slc + 1) * 512], AF.Square, waits=w, inc=sc)
            w = [(sc, sc.n)]
            if slc == 0:
                w.append((sv, memset_done))
                w.append((sv, sv_qn_ms[max(sv_qn_ms)]))
            else:
                w.append((sv, sv.n))
            op("t", "matmul", pb[7][0:1, :], ones_cf[:, :], Df[:, 1024:1536],
               start=True, stop=True, waits=w, inc=st)
            st_kn[slc] = st.n
            op("v", "reduce_max", kn_t[0:1, slc:slc + 1], pb[7][0:1, :], AX.X,
               waits=[(st, st_kn[slc])], inc=sv)
        st_kn15 = st_kn[15]
        op("s", "activation", kn_t2[0:1, :], kn_t[0:1, :], AF.Identity,
           waits=[(sv, sv.n)], inc=sc)
        op("v", "reduce_max", sc1[0:1, 8:9], kn_t2[0:1, :], AX.X,
           waits=[(sc, sc.n)], inc=sv)
        op("s", "activation", sc1[0:1, 10:11], sc1[0:1, 8:9], AF.Identity,
           waits=[(sv, sv.n)], inc=sc)
        op("v", "tensor_scalar", mg2_t[0:1, :], qn_t[0:1, :], sc1[0:1, 10:11],
           None, OP.mult, waits=[(sc, sc.n)], inc=sv)
        op("s", "activation", sqrt_t[0:1, :], mg2_t[0:1, :], AF.Sqrt,
           waits=[(sv, sv.n)], inc=sc)
        op("v", "tensor_scalar", negmg_t[0:1, :], sqrt_t[0:1, :], -SCALE,
           None, OP.mult, waits=[(sc, sc.n)], inc=sv)
        op("t", "matmul", pb[7][:, 0:64], ones_rf[:, :], negmg_t[0:1, :],
           start=True, stop=True, waits=[(sv, sv.n)], inc=st)
        op("v", "tensor_copy", negmg_sb[:, :], pb[7][:, 0:64],
           waits=[(st, st.n)], inc=sv)
        sv_kmax = sv.n
        if debug:
            dma("y", dbg_kv.ap(), Cbf[:, :],
                waits=[(sv, sv_kv_all)], inc=yout)
            dma("y", dbg_xgs.ap(), Bbf[:, 0:16384],
                waits=[(gxg[0], 64)], inc=yout)
            dma("y", dbg_g01.ap(), g01_sb[0:40, :], waits=[(sv, sv_g01)], inc=yout)
            dma("y", dbg_enc.ap()[:, 0:1280], encT_sb[:, :], waits=[(sv, sv_encT)], inc=yout)
            dma("y", dbg_w01.ap(), w01s_sb[:, :], waits=[(sv, sv_w01)], inc=yout)

        def p2_sb(jj):
            return Dbf[:, (jj % 2) * 1024:((jj % 2) + 1) * 1024]

        def qt_sl(p):
            return Dbf[:, 2048 + (p % 2) * 512:2048 + ((p % 2) + 1) * 512]

        def ot_sl(p):
            return Dbf[:, 3072 + (p % 2) * 512:3072 + ((p % 2) + 1) * 512]

        bc_sb = Df[:, 3072:3584]

        gwo_ms = {}
        for ocg in range(2):
            wsl = ocg % 2
            w = [(st, st_idx_cons[15])] if ocg == 0 else []
            for sub in range(4):
                rows = wo[sub * 1024:(sub + 1) * 1024, ocg * 512:(ocg + 1) * 512]
                dma("g", Abf[:, wsl * 16384 + sub * 4096: wsl * 16384 + (sub + 1) * 4096]
                    .rearrange("p (dq t) -> p dq t", dq=8),
                    rows.rearrange("(dq p) t -> p dq t", p=128),
                    waits=w if sub == 0 else (), inc=gwo[wsl])
                w = []
            gwo_ms[ocg] = gwo[wsl].n

        def a2a_h(h, waitval):
            def fn(eng, h=h, waitval=waitval):
                eng.wait_ge(yot[h].h, waitval)
                with allow_ici_all_to_all():
                    eng.collective_compute(
                        "AllToAll", OP.bypass,
                        replica_groups=[list(range(NCORES))],
                        ins=[oa_in[h].ap().opt()], outs=[oa_out[h].ap().opt()],
                    ).then_inc(cc2[h].h, 1)
            emit("g", fn)
            cc2[h].n = 1

        st_qk = {}
        st_sums = {}
        sc_exp2 = {}
        sv_omult = {}
        st_lastqk = {}
        sc_bc = {}
        sv_negmg = {}
        ot_dma = {}
        sv_recip = {}
        st_bc = {}
        gk = 0
        order = [(i, qh) for i in range(NCHUNK) for qh in range(4)]
        epilog_done = set()

        def flush_epilog(s2, force=False):
            tgt = s2 if force else s2 - 1
            if tgt < 0 or tgt in epilog_done:
                return
            epilog_done.add(tgt)
            i2, qh2 = order[tgt]
            ob2 = 4 + tgt % 2
            sb2 = 6 + tgt % 2
            w = [(sv, sv_recip[tgt])]
            op("t", "matmul", pb[sb2][:, :], ones_rf[:, :], recip_sb[0:1, :],
               start=True, stop=True, waits=w, inc=st)
            st_bc[tgt] = st.n
            op("s", "activation", bc_sb, pb[sb2][:, :], AF.Identity,
               waits=[(st, st_bc[tgt])], inc=sc)
            sc_bc[tgt] = sc.n
            w = [(sc, sc_bc[tgt])]
            if tgt >= 2:
                w.append(ot_dma[tgt - 2])
            op("v", "tensor_tensor", ot_sl(tgt), pb[ob2][:, :], bc_sb,
               OP.mult, waits=w, inc=sv)
            sv_omult[tgt] = sv.n
            d_i, h_i = chunk_rank(i2), chunk_half(i2)
            dma("y", oa_in[h_i].ap()[d_i * 512 + qh2 * 128:
                                     d_i * 512 + (qh2 + 1) * 128, :],
                ot_sl(tgt), waits=[(sv, sv_omult[tgt])], inc=yot[h_i])
            ot_dma[tgt] = (yot[h_i], yot[h_i].n)
            if tgt == 31:
                a2a_h(0, yot[0].n)
        for seq, (i, qh) in enumerate(order):
            pairid = qh * 16 + i
            psl = seq % 2
            ob = 4 + seq % 2
            sb_ = 6 + seq % 2
            nk = 4 * i + 4
            w = []
            if seq < 2:
                w.append((yqtw2[0], yqtw2[0].n))
                w.append((yqtw2[1], yqtw2[1].n))
                w.append((st, st_kn15))
            if seq >= 2:
                w.append((st, st_lastqk[seq - 2]))
            dma("y", qt_sl(seq),
                qt_d.ap()[qh * 128:(qh + 1) * 128, i * 512:(i + 1) * 512],
                waits=w, inc=yqt[psl])

            g0 = gk
            nd = nk // 2

            def emit_qk2(jj, g0=g0, seq=seq, psl=psl, i=i):
                for hh in (0, 1):
                    j = 2 * jj + hh
                    g = g0 + j
                    dst = pq[jj % 2][:, hh * 512:(hh + 1) * 512]
                    w = []
                    if jj <= 1 and hh == 0:
                        if jj * 2 + g0 >= 4:
                            w.append((sc, sc_exp2[(g0 + 2 * jj) // 2 - 2]))
                        if seq == 0 and jj == 1:
                            w.append((sv, sv_kmax))
                        if j == 0:
                            w.append((yqt[psl], yqt[psl].n))
                            if seq >= 2:
                                w.append((sv, sv_omult[seq - 2]))
                            if seq == 0:
                                w.append((sv, sv_kv_all))
                    diag = (j >= 4 * i)
                    op("t", "matmul", dst, Kt(j), qt_sl(seq),
                       start=True, stop=(not diag), waits=w,
                       inc=None if diag else st)
                    if diag:
                        v_ = j - 4 * i
                        op("t", "matmul", dst, I_bf[:, :],
                           maskc_sb[:, v_ * 512:(v_ + 1) * 512],
                           start=False, stop=True, inc=st)
                    st_qk[g] = st.n

            emit_qk2(0)
            if nd > 1:
                emit_qk2(1)
            for jj in range(nd):
                g2 = g0 // 2 + jj
                gA = g0 + 2 * jj
                gB = gA + 1
                w = [(st, st_qk[gB])]
                if seq == 0 and jj == 0:
                    w.append((sv, sv_kmax))
                if g2 >= 2:
                    w.append((st, st_sums.get(g0 + 2 * (jj - 2) + 1, 0)))
                op("s", "activation", p2_sb(jj), pq[jj % 2][:, :], AF.Exp,
                   bias=negmg_sb[:, pairid:pairid + 1], scale=SCALE,
                   waits=w, inc=sc)
                sc_exp2[g2] = sc.n
                w = [(sc, sc_exp2[g2])]
                op("t", "matmul", pb[ob][:, :], Vt_tile(2 * jj),
                   p2_sb(jj)[:, 0:512], start=(jj == 0), stop=False,
                   skip_group_check=True, waits=w)
                op("t", "matmul", pb[ob][:, :], Vt_tile(2 * jj + 1),
                   p2_sb(jj)[:, 512:1024], start=False, stop=(jj == nd - 1),
                   skip_group_check=True)
                op("t", "matmul", pb[sb_][0:1, :], ones_cb[:, :],
                   p2_sb(jj)[:, 0:512], start=(jj == 0), stop=False,
                   skip_group_check=True)
                op("t", "matmul", pb[sb_][0:1, :], ones_cb[:, :],
                   p2_sb(jj)[:, 512:1024], start=False, stop=(jj == nd - 1),
                   skip_group_check=True, inc=st)
                st_sums[gB] = st.n
                if jj + 2 < nd:
                    emit_qk2(jj + 2)
                if jj == 0:
                    flush_epilog(seq)
            gk = g0 + nk
            st_lastqk[seq] = st_qk[gk - 1]

            w = [(st, st_sums[gk - 1])]
            if seq >= 1:
                w.append((st, st_bc[seq - 1]))
            op("v", "reciprocal", recip_sb[0:1, :], pb[sb_][0:1, :],
               waits=w, inc=sv)
            sv_recip[seq] = sv.n
        flush_epilog(len(order) - 1, force=True)
        a2a_h(1, yot[1].n)
        st_lastP1 = st.n
        sc_lastP1 = sc.n
        sv_lastP1 = sv.n

        x_f = lambda p: Df[:, (p % 2) * 512:((p % 2) + 1) * 512]
        h1p_t = lambda p: Df[:, 1024 + (p % 2) * 512:1024 + ((p % 2) + 1) * 512]
        sqs = Df[:, 2048:2560]

        st_ymm = {}
        sv_yevac = {}
        sc_ysq = {}
        for half in range(2):
            w = [(cc2[half], 1)]
            if half == 0:
                w.append((st, st_lastP1))
            else:
                w.append((st, st_ymm[(0, 7, 3)]))
            for sub in range(4):
                rows = oa_out[half].ap()[sub * 1024:(sub + 1) * 1024, :]
                dma("y", Cbf[:, sub * 4096:(sub + 1) * 4096]
                    .rearrange("p (fq t) -> p fq t", fq=8),
                    rows.rearrange("(fq p) t -> p fq t", p=128),
                    waits=w if sub == 0 else (), inc=yotl)
                w = []
            ot_l = yotl.n
            for oc in range(8):
                ocg = half * 8 + oc
                wsl = ocg % 2
                if ocg >= 2:
                    w = [(st, st_ymm[(half, oc - 2, 3)] if oc >= 2
                          else st_ymm[(half - 1, oc + 6, 3)])]
                    for sub in range(4):
                        rows = wo[sub * 1024:(sub + 1) * 1024,
                                  oc * 512:(oc + 1) * 512]
                        dma("g", Abf[:, wsl * 16384 + sub * 4096:
                                     wsl * 16384 + (sub + 1) * 4096]
                            .rearrange("p (dq t) -> p dq t", dq=8),
                            rows.rearrange("(dq p) t -> p dq t", p=128),
                            waits=w if sub == 0 else (), inc=gwo[wsl])
                        w = []
                    gwo_ms[ocg] = gwo[wsl].n
                for tt in range(4):
                    tg = half * 4 + tt
                    it = ocg * 4 + tt
                    par = it % 2
                    w = []
                    if it < 2:
                        w.append((st, st_lastP1))
                    if it >= 2:
                        w.append((sv, sv_yevac[it - 2]))
                    dma("y", x_f(par),
                        x_my[tg * 128:(tg + 1) * 128, oc * 512:(oc + 1) * 512],
                        waits=w, inc=yx[par])
                    bank = tt % 2
                    w = [(gwo[wsl], gwo_ms[ocg]), (yotl, ot_l)]
                    if it >= 2:
                        w.append((sv, sv_yevac[it - 2]))
                    if ocg == 0 and tt < 2:
                        w.append((sc, sc_lastP1))
                    for fq in range(32):
                        op("t", "matmul", pb[bank][:, :],
                           Cbf[:, fq * 512 + tt * 128: fq * 512 + (tt + 1) * 128],
                           Abf[:, wsl * 16384 + fq * 512:
                               wsl * 16384 + (fq + 1) * 512],
                           start=(fq == 0), stop=(fq == 31),
                           waits=w if fq == 0 else (), inc=st if fq == 31 else None)
                    st_ymm[(half, oc, tt)] = st.n
                    w = [(st, st_ymm[(half, oc, tt)]), (yx[par], yx[par].n)]
                    if it >= 2:
                        w.append((yh1prew, yh1prew.n))
                        w.append((sc, sc_ysq.get(it - 2, 0)))
                    op("v", "tensor_tensor", h1p_t(par), pb[bank][:, :],
                       x_f(par), OP.add, waits=w, inc=sv)
                    sv_yevac[it] = sv.n
                    col = tg * 8 + oc
                    w = [(sv, sv_yevac[it]), (sc, sc_ysq.get(it - 1, 0))]
                    op("s", "activation", sqs, h1p_t(par), AF.Identity,
                       accum_out=stats1[:, col:col + 1], waits=w, inc=sc)
                    op("s", "activation", sqs, h1p_t(par), AF.Square,
                       accum_out=stats2[:, col:col + 1], inc=sc)
                    sc_ysq[it] = sc.n
                    dma("y", h1pre_d.ap()[tg * 128:(tg + 1) * 128,
                                          oc * 512:(oc + 1) * 512],
                        h1p_t(par), waits=[(sv, sv_yevac[it])], inc=yh1prew)
        yh1prew_all = yh1prew.n


        def ln_batch_finalize(first_waits):
            w = list(first_waits)
            for tg in range(8):
                op("v", "tensor_reduce", lns[:, tg:tg + 1],
                   stats1[:, tg * 8:(tg + 1) * 8], AX.X, OP.add,
                   waits=w if tg == 0 else ())
                w = []
                op("v", "tensor_reduce", lns[:, 8 + tg:9 + tg],
                   stats2[:, tg * 8:(tg + 1) * 8], AX.X, OP.add,
                   inc=sv if tg == 7 else None)
            op("s", "mul", lns[:, 16:24], lns[:, 0:8], 1.0 / D,
               waits=[(sv, sv.n)], inc=sc)
            op("s", "mul", lns[:, 24:32], lns[:, 8:16], 1.0 / D, inc=sc)
            op("s", "activation", lns[:, 32:40], lns[:, 16:24], AF.Square,
               inc=sc)
            op("v", "scalar_tensor_tensor", lns[:, 40:48], lns[:, 24:32], EPS,
               lns[:, 32:40], OP.add, OP.subtract, waits=[(sc, sc.n)], inc=sv)
            op("s", "activation", lns[:, 48:56], lns[:, 40:48], AF.Sqrt,
               waits=[(sv, sv.n)], inc=sc)
            op("v", "reciprocal", lns[:, 56:64], lns[:, 48:56],
               waits=[(sc, sc.n)], inc=sv)
            op("s", "activation", lns[:, 64:72], lns[:, 56:64], AF.Identity,
               waits=[(sv, sv.n)], inc=sc)
            op("s", "mul", lns[:, 72:80], lns[:, 16:24], -1.0, inc=sc)
            op("v", "tensor_tensor", lns[:, 80:88], lns[:, 64:72],
               lns[:, 72:80], OP.mult, waits=[(sc, sc.n)], inc=sv)
            op("s", "activation", lns[:, 88:89], lns[:, 80:81], AF.Identity,
               waits=[(sv, sv.n)], inc=sc)
            return sc.n, sv.n
        h1in = lambda p: Af[:, (p % 2) * 4096:((p % 2) + 1) * 4096]
        h1o = lambda p: Af[:, 8192 + (p % 2) * 4096:8192 + ((p % 2) + 1) * 4096]
        h1bf = lambda p: (Dbf[:, 0:4096] if p % 2 == 0 else Cbf[:, 0:4096])

        sc_fin1, sv_fin1 = ln_batch_finalize(
            [(sc, sc_ysq[63]), (sv, sv_lastP1)])
        sc_apply = {}
        st_h1t = {}
        sv_h1grp = {}
        for tg in range(8):
            par = tg % 2
            w = [(yh1prew, yh1prew_all)]
            if tg == 0:
                w.append((st, st_ymm[(1, 7, 3)]))
            if tg >= 2:
                w.append((sc, sc_apply[tg - 2]))
            dma("y", h1in(par), h1pre_d.ap()[tg * 128:(tg + 1) * 128, :],
                waits=w, inc=yh1pr[par])
            w = [(sv, sv_fin1), (sc, sc_fin1), (yh1pr[par], yh1pr[par].n)]
            if tg >= 2:
                w.append((sv, sv_h1grp.get((tg - 2, 7), 0)))
                w.append((st, st_h1t[tg - 2]))
            op("s", "activation", h1bf(tg), h1in(par), AF.Identity,
               bias=lns[:, 80 + tg:81 + tg], scale=lns[:, 64 + tg:65 + tg],
               waits=w, inc=sc)
            sc_apply[tg] = sc.n
            sv_h1bf = sc_apply[tg]
            for gg in range(8):
                bank = 2 + gg % 2
                tgg = tg * 8 + gg
                w = [(sc, sc_apply[tg])]
                if tgg >= 2:
                    w.append((sv, sv_h1grp[(tgg - 2) // 8, (tgg - 2) % 8]))
                if tgg < 2:
                    w.append((sv, sv_omult[63]))
                for k in range(4):
                    dq = gg * 4 + k
                    op("t", "matmul", pb[bank][:, k * 128:(k + 1) * 128],
                       h1bf(tg)[:, dq * 128:(dq + 1) * 128], I_bf[:, :],
                       start=True, stop=True, waits=w if k == 0 else (),
                       inc=st if k == 3 else None)
                stg = st.n
                dstap = B3[:, gg * 4:(gg + 1) * 4, tg * 128:(tg + 1) * 128]
                op("v", "tensor_copy", dstap, pb[bank][:, :],
                   waits=[(st, stg)], inc=sv)
                sv_h1grp[(tg, gg)] = sv.n
            st_h1t[tg] = st.n
        sv_h1T_all = sv.n

        h1s_t = lambda p: Df[:, (p % 2) * 512:((p % 2) + 1) * 512]
        h2_t = lambda p: Df[:, 1024 + (p % 2) * 512:1024 + ((p % 2) + 1) * 512]
        h1f_t = lambda p: Df[:, 2560 + (p % 2) * 512:2560 + ((p % 2) + 1) * 512]

        st_e01 = {}
        sv_e01l = 0
        for tc2 in range(2):
            for dq in range(32):
                w = []
                if dq == 0 and tc2 == 0:
                    w.append((sv, sv_h1T_all))
                if dq == 0 and tc2 == 1:
                    w.append((sv, sv_e01l))
                op("t", "matmul", pb[4][0:40, :],
                   encT_sb[:, dq * 40:(dq + 1) * 40],
                   Bbf[:, dq * 1024 + tc2 * 512: dq * 1024 + (tc2 + 1) * 512],
                   start=(dq == 0), stop=(dq == 31),
                   waits=w, inc=st if dq == 31 else None)
            st_e01[tc2] = st.n
            op("v", "tensor_copy", e01T_sb[0:40, tc2 * 512:(tc2 + 1) * 512],
               pb[4][0:40, :], waits=[(st, st_e01[tc2])], inc=sv)
            sv_e01l = sv.n
            sv_e01_final[0] = sv.n
        sv_e01 = sv.n

        st_base = {}
        sv_h2evac = {}
        sc_h2sq = {}
        st_wbt = {}
        sv_wbtevac = {}
        for oc in range(8):
            wsl = oc % 2
            w = []
            if oc == 0:
                w.append((sc, sc_apply[7]))
            if oc == 1:
                w.append((sv, sv_h1T_all))
            if oc >= 2:
                w.append((st, st_wbt[oc - 2]))
            dma("g", Abf[:, wsl * 16384:(wsl + 1) * 16384]
                .rearrange("p (ot d) -> p ot d", ot=4),
                wbase[oc * 512:(oc + 1) * 512, :]
                .rearrange("(ot p) d -> p ot d", p=128),
                waits=w, inc=gwb[wsl])
            for dq in range(32):
                bank = 2 + dq % 2
                w = [(gwb[wsl], gwb[wsl].n)]
                if oc == 0 and dq == 0:
                    w.append((sv, sv_e01))
                if oc >= 1 and dq == 0:
                    w.append((st, st_base[(oc - 1, 7)]))
                if dq >= 2:
                    w.append((sv, sv_wbtevac[(oc, dq - 2)]))
                for k in range(4):
                    op("t", "matmul", pb[bank][:, k * 128:(k + 1) * 128],
                       Abf[:, wsl * 16384 + k * 4096 + dq * 128:
                           wsl * 16384 + k * 4096 + (dq + 1) * 128],
                       I_bf[:, :], start=True, stop=True,
                       waits=w if k == 0 else (), inc=st if k == 3 else None)
                stg = st.n
                op("v", "tensor_copy", Cbf[:, dq * 512:(dq + 1) * 512],
                   pb[bank][:, :], waits=[(st, stg)], inc=sv)
                sv_wbtevac[(oc, dq)] = sv.n
            st_wbt[oc] = st.n
            sv_wbt_all = sv.n
            for tt in range(8):
                it = oc * 8 + tt
                par = it % 2
                w = []
                if it == 0:
                    w.append((yh1prew, yh1prew.n))
                if it < 2:
                    w.append((st, st_h1t[7]))
                if it >= 2:
                    w.append((sv, sv_h2evac[it - 2]))
                dma("y", h1s_t(par),
                    h1pre_d.ap()[tt * 128:(tt + 1) * 128, oc * 512:(oc + 1) * 512],
                    waits=w, inc=yh1s[par])
                w = [(yh1s[par], yh1s[par].n)]
                if it >= 2:
                    w.append((sv, sv_h2evac[it - 2]))
                op("s", "activation", h1f_t(par), h1s_t(par), AF.Identity,
                   bias=lns[:, 80 + tt:81 + tt], scale=lns[:, 64 + tt:65 + tt],
                   waits=w, inc=sc)
                sc_h1f = sc.n
                bank = tt % 2
                w = [(sv, sv_wbt_all)]
                if it >= 2:
                    w.append((sv, sv_h2evac[it - 2]))
                for dq in range(32):
                    op("t", "matmul", pb[bank][:, :],
                       Bbf[:, dq * 1024 + tt * 128: dq * 1024 + (tt + 1) * 128],
                       Cbf[:, dq * 512:(dq + 1) * 512],
                       start=(dq == 0), stop=False,
                       waits=w if dq == 0 else ())
                op("t", "matmul", pb[bank][:, :], ones_rb[:, :],
                   b_bf[0:1, oc * 512:(oc + 1) * 512], start=False, stop=False)
                op("t", "matmul", pb[bank][:, :],
                   e01T_sb[0:40, tt * 128:(tt + 1) * 128],
                   g01_sb[0:40, oc * 512:(oc + 1) * 512],
                   start=False, stop=True, inc=st)
                st_base[(oc, tt)] = st.n
                w = [(st, st_base[(oc, tt)]), (sc, sc_h1f)]
                if it >= 2:
                    w.append((yh2w, yh2w.n))
                    w.append((sc, sc_h2sq.get(it - 2, 0)))
                op("v", "tensor_tensor", h2_t(par), pb[bank][:, :], h1f_t(par),
                   OP.add, waits=w, inc=sv)
                sv_h2evac[it] = sv.n
                col = tt * 8 + oc
                w = [(sv, sv_h2evac[it]), (sc, sc_h2sq.get(it - 1, 0))]
                op("s", "activation", sqs, h2_t(par), AF.Identity,
                   accum_out=stats1[:, col:col + 1], waits=w, inc=sc)
                op("s", "activation", sqs, h2_t(par), AF.Square,
                   accum_out=stats2[:, col:col + 1], inc=sc)
                sc_h2sq[it] = sc.n
                dma("y", h2_d.ap()[tt * 128:(tt + 1) * 128,
                                   oc * 512:(oc + 1) * 512],
                    h2_t(par), waits=[(sv, sv_h2evac[it])], inc=yh2w)
        yh2w_all = yh2w.n

        h2in = lambda p: Af[:, (p % 4) * 4096:((p % 4) + 1) * 4096]
        outsb = lambda p: Cf[:, (p % 2) * 4096:((p % 2) + 1) * 4096]
        sc_fin2, sv_fin2 = ln_batch_finalize(
            [(sc, sc_h2sq[63]), (sc, sc_apply[7])])
        sc_ap2 = {}
        for tg in range(8):
            par = tg % 4
            w = [(yh2w, yh2w_all)]
            if tg == 0:
                w.append((gwb[0], gwb[0].n))
                w.append((gwb[1], gwb[1].n))
            if tg >= 4:
                w.append((sc, sc_ap2[tg - 4]))
            dma("g", h2in(par), h2_d.ap()[tg * 128:(tg + 1) * 128, :],
                waits=w, inc=yh2r[par])
            w = [(sv, sv_fin2), (sc, sc_fin2), (yh2r[par], yh2r[par].n)]
            if tg >= 2:
                w.append((yout, 16 * (tg - 1)))
            op("s", "activation", outsb(tg), h2in(par), AF.Identity,
               bias=lns[:, 80 + tg:81 + tg], scale=lns[:, 64 + tg:65 + tg],
               waits=w, inc=sc)
            sc_ap2[tg] = sc.n
            dma("y", out_my[tg * 128:(tg + 1) * 128, :], outsb(tg),
                waits=[(sc, sc_ap2[tg])], inc=yout)
        if debug:
            dma("y", dbg_xg.ap()[:, 0:512], xg[0].ap(),
                waits=[(cc1, 2)], inc=yout)
            dma("y", dbg_xg.ap()[:, 512:1024], xg[1].ap(),
                waits=[(cc1, 2)], inc=yout)
            dma("y", dbg_oin.ap()[:, 0:512], oa_in[0].ap(),
                waits=[(yot[0], yot[0].n)], inc=yout)
            dma("y", dbg_oin.ap()[:, 512:1024], oa_in[1].ap(),
                waits=[(yot[1], yot[1].n)], inc=yout)
            dma("y", dbg_oout.ap()[:, 0:512], oa_out[0].ap(),
                waits=[(cc2[0], 1)], inc=yout)
            dma("y", dbg_oout.ap()[:, 512:1024], oa_out[1].ap(),
                waits=[(cc2[1], 1)], inc=yout)
            dma("y", dbg_enc.ap()[0:40, 1280:2304], e01T_sb[0:40, :],
                waits=[(sv, lambda: sv_e01_final[0])], inc=yout)
        yout_all = yout.n

        def finwait(eng):
            eng.wait_ge(yout.h, yout_all)
        emit("y", finwait)

        with nc.Block() as block:
            @block.tensor
            def _(t):
                for fn in EM["t"]:
                    fn(t)

            @block.vector
            def _(v):
                for fn in EM["v"]:
                    fn(v)

            @block.scalar
            def _(s):
                for fn in EM["s"]:
                    fn(s)

            @block.gpsimd
            def _(g):
                for fn in EM["g"]:
                    fn(g)

            @block.sync
            def _(y):
                for fn in EM["y"]:
                    fn(y)

    return nc


_CACHE = {}


def _build_ft():
    if "ft" in _CACHE:
        return _CACHE["ft"]
    n = D
    t = np.arange(n)[:, None].astype(np.float64)
    F = np.zeros((n, n), dtype=np.float64)
    F[:, 0] = 1.0 / math.sqrt(n)
    f = np.arange(1, (n - 1) // 2 + 1)[None, :].astype(np.float64)
    ang = 2.0 * np.pi * t * f / n
    F[:, 1:1 + 2 * ((n - 1) // 2):2] = math.sqrt(2.0 / n) * np.cos(ang)
    F[:, 2:2 + 2 * ((n - 1) // 2):2] = math.sqrt(2.0 / n) * np.sin(ang)
    F[:, n - 1] = ((-1.0) ** t[:, 0]) / math.sqrt(n)
    FT = np.ascontiguousarray(F.T.astype(np.float32))
    _CACHE["ft"] = FT
    return FT


def _build_maskc():
    kk = np.arange(128)[:, None]
    qq = np.arange(512)[None, :]
    out = np.zeros((128, 4 * 512), dtype=np.float32)
    for v in range(4):
        out[:, v * 512:(v + 1) * 512] = np.where(qq < kk + 128 * v, NEG, 0.0)
    return out.astype(ml_dtypes.bfloat16)


def kernel(L0_attn_x, L0_attn_wq, L0_attn_wk, L0_attn_wv, L0_attn_wo,
           L0_ffn_W_base, L0_ffn_b_base, L0_ffn_enc0, L0_ffn_enc1,
           L0_ffn_coefs0, L0_ffn_coefs1, L0_ffn_mask0, L0_ffn_mask1):
    x = np.asarray(L0_attn_x, dtype=np.float32).reshape(S, D)
    wq = np.asarray(L0_attn_wq, dtype=np.float32)
    wk = np.asarray(L0_attn_wk, dtype=np.float32)
    wv = np.asarray(L0_attn_wv, dtype=np.float32)
    wo_f = np.ascontiguousarray(np.asarray(L0_attn_wo, dtype=np.float32))
    wb_f = np.ascontiguousarray(np.asarray(L0_ffn_W_base, dtype=np.float32))
    bb = np.ascontiguousarray(np.asarray(L0_ffn_b_base, dtype=np.float32).reshape(1, D))
    enc01 = np.ascontiguousarray(
        np.concatenate([np.asarray(L0_ffn_enc0, np.float32),
                        np.asarray(L0_ffn_enc1, np.float32)], axis=0))
    c01 = np.concatenate([np.asarray(L0_ffn_coefs0, np.float32),
                          np.asarray(L0_ffn_coefs1, np.float32)], axis=2)
    m01 = np.concatenate([np.asarray(L0_ffn_mask0), np.asarray(L0_ffn_mask1)],
                         axis=2).astype(np.uint8)
    coefs_st = np.ascontiguousarray(
        c01.reshape(3, 32, 128, 40).transpose(2, 0, 1, 3).reshape(128, 3840))
    masks_st = np.ascontiguousarray(
        m01.reshape(3, 32, 128, 40).transpose(2, 0, 1, 3).reshape(128, 3840))
    ft = _build_ft().astype(ml_dtypes.bfloat16)
    ident = np.eye(128, dtype=np.float32).astype(ml_dtypes.bfloat16)
    maskc = _build_maskc()

    if "nc" not in _CACHE:
        _CACHE["nc"] = build_program()
    nc = _CACHE["nc"]

    in_maps = []
    for c in range(NCORES):
        lo, hi = c, NCHUNK - 1 - c
        x_my = np.ascontiguousarray(
            np.concatenate([x[lo * 512:(lo + 1) * 512, :],
                            x[hi * 512:(hi + 1) * 512, :]], axis=0))
        in_maps.append(dict(
            x_my=x_my,
            wq_my=np.ascontiguousarray(wq[:, c * 512:(c + 1) * 512]),
            wk_my=np.ascontiguousarray(wk[:, c * 128:(c + 1) * 128]),
            wv_my=np.ascontiguousarray(wv[:, c * 128:(c + 1) * 128]),
            wo=wo_f, wbase=wb_f, bbase=bb, enc01=enc01,
            coefs=coefs_st, masks=masks_st, ft=ft, ident=ident, maskc=maskc,
        ))
    res = run_bass_kernel_spmd(nc, in_maps, core_ids=list(range(NCORES)))
    out = np.empty((S, D), dtype=np.float32)
    for c in range(NCORES):
        o = res.results[c]["out_my"]
        lo, hi = c, NCHUNK - 1 - c
        out[lo * 512:(lo + 1) * 512, :] = o[0:512, :]
        out[hi * 512:(hi + 1) * 512, :] = o[512:1024, :]
    return out.reshape(1, S, D)
